# version 20
# baseline (speedup 1.0000x reference)
import numpy as np
import concourse.bass as bass
import concourse.mybir as mybir
from concourse.bass_utils import run_bass_kernel_spmd

F32 = mybir.dt.float32
BF16 = mybir.dt.bfloat16
U8 = mybir.dt.uint8
AF = mybir.ActivationFunctionType
OP = mybir.AluOpType

P = 128
D = 1024
KC = 8
T = 2048
TT = 512
HW = 256
EPS = 1e-5

C_FB = 0
C_BCONV = 8
C_WCONV = 16
C_BGF = 48
C_BGI = 56
C_SELB = 64
C_CMPB = 128
C_M8 = 192
C_M16 = 193
C_INVD = 194
C_EPS = 195
C_MH = 196
C_M1 = 197
C_H = 198
VW = 200


class Ctl:
    def __init__(self, nc, es):
        self.nc = nc
        self.h = {}
        self.cnt = {}
        self.waited = {}
        for name in ("wd0a", "wd0b", "wd1a", "wd1b",
                     "gw0", "gw1", "gw2", "wds0", "wds1", "wds2",
                     "gx", "gr", "go", "gv",
                     "pe", "sc", "ve", "gpc"):
            self.h[name] = es.enter_context(nc.semaphore(name))
            self.cnt[name] = 0

    def need(self, eng, engname, sem, val):
        if val is None or val <= 0:
            return
        key = (engname, sem)
        if self.waited.get(key, 0) >= val:
            return
        assert val <= self.cnt[sem], (
            f"wait {sem}>={val} but only {self.cnt[sem]} incs emitted")
        eng.wait_ge(self.h[sem], val)
        self.waited[key] = val

    def needt(self, eng, engname, tok):
        if tok is not None:
            self.need(eng, engname, tok[0], tok[1])

    def inc(self, instr, sem, n=1):
        instr.then_inc(self.h[sem], n)
        self.cnt[sem] += n
        return self.cnt[sem]


class Chained:

    def __init__(self, eng, engname, sem, ctl):
        self._eng = eng
        self._engname = engname
        self._sem = sem
        self._ctl = ctl

    def __getattr__(self, name):
        fn = getattr(self._eng, name)
        if name in ("wait_ge", "nop"):
            return fn

        def wrapped(*a, **k):
            ctl = self._ctl
            ctl.need(self._eng, self._engname, self._sem, ctl.cnt[self._sem])
            return fn(*a, **k)
        return wrapped


def build(nt=4):
    nc = bass.Bass("TRN2", target_bir_lowering=False, debug=False)

    xT = nc.dram_tensor("xT", [D, T], F32, kind="ExternalInput").ap()
    w1T = nc.dram_tensor("w1T", [D, 2048], F32, kind="ExternalInput").ap()
    w2T = nc.dram_tensor("w2T", [D, 2048], F32, kind="ExternalInput").ap()
    w3T = nc.dram_tensor("w3T", [D, 1024], F32, kind="ExternalInput").ap()
    selT = nc.dram_tensor("selT", [D, 8192], F32, kind="ExternalInput").ap()
    cmpT = nc.dram_tensor("cmpT", [D, 8192], F32, kind="ExternalInput").ap()
    vecs_d = nc.dram_tensor("vecs", [P, VW], F32, kind="ExternalInput").ap()
    outT = nc.dram_tensor("outT", [D, T], F32, kind="ExternalOutput").ap()
    wscr = nc.dram_tensor("wscr", [84 * P, 2 * 8 * 256], BF16).ap()

    half_list = []
    for s in (2, 3, 0, 1):
        half_list.append((w1T, s * TT))
        half_list.append((w1T, s * TT + HW))
    for p2 in range(4):
        half_list.append((w2T, p2 * HW))
        half_list.append((w2T, 1024 + p2 * HW))
    for i in range(8):
        for s in (2 * i, 2 * i + 1):
            half_list.append((selT, s * TT))
            half_list.append((selT, s * TT + HW))
        for s in (2 * i, 2 * i + 1):
            half_list.append((cmpT, s * TT))
            half_list.append((cmpT, s * TT + HW))
    for s in range(2):
        half_list.append((w3T, s * TT))
        half_list.append((w3T, s * TT + HW))
    NH = len(half_list)

    from contextlib import ExitStack
    es = ExitStack()
    with es:
        sb = lambda n, s, d=F32: es.enter_context(nc.sbuf_tensor(n, s, d))
        w32 = [sb("w320", [P, 8 * HW]), sb("w321", [P, 8 * HW])]
        wh = [sb("wh0", [P, 8 * HW], BF16), sb("wh1", [P, 8 * HW], BF16),
              sb("wh2", [P, 8 * HW], BF16)]
        wl = [sb("wl0", [P, 8 * HW], BF16), sb("wl1", [P, 8 * HW], BF16),
              sb("wl2", [P, 8 * HW], BF16)]
        xnh = sb("xnh", [P, 8 * TT], BF16)
        xnl = sb("xnl", [P, 8 * TT], BF16)
        xch = sb("xch", [P, 8 * TT], BF16)
        xcl = sb("xcl", [P, 8 * TT], BF16)
        gg = sb("gg", [P, 8 * TT])
        sf = sb("sf", [P, 8 * TT])
        si = sb("si", [P, 8 * TT])
        al = sb("al", [P, 8 * TT])
        hb = sb("hb", [P, 8 * TT])
        XAW = 520
        xa = sb("xa", [P, 8 * XAW])
        vecs = sb("vecsb", [P, VW])
        ce = sb("ce", [P, KC])
        ce2 = sb("ce2", [P, KC])
        hcarry = sb("hcarry", [P, KC])
        rows = sb("rows", [1, 4 * TT])
        poolA = sb("poolA", [P, 2 * TT])
        poolB = sb("poolB", [P, 2 * TT])
        poolC = sb("poolC", [P, 2 * TT])
        scr = sb("scr", [P, 2 * TT])
        scr2 = sb("scr2", [P, 2 * TT])
        mxb = sb("mxb", [P, TT])
        accb = sb("accb", [P, TT])
        msk = sb("msk", [P, 8 * TT], U8)
        ones_c = sb("ones_c", [1, P])
        ones_r = sb("ones_r", [P, 1])

        pg = [es.enter_context(nc.psum_tensor(f"pg{i}", [P, TT], F32))
              for i in range(8)]

        c = Ctl(nc, es)
        block = es.enter_context(nc.Block())

        free_xn = [None]
        free_gg = [None] * KC
        free_hb = [None] * KC
        free_sf = [None] * KC
        free_si = [None]
        free_al = [None] * KC
        free_xa = [None] * KC
        free_xc_pe = [None]
        free_y_pe = [None]
        u_toks = [None] * KC
        free_A = [None, None]
        free_B = [None, None]
        free_C = [None, None]
        free_scr = [None, None]
        free_scr2 = [None, None]
        free_rows = [None]
        bank_free = [None] * 8
        whl_free = [None, None, None]
        w32_free = [[], []]
        w_ready = [[], [], []]
        wd_loaded = [0, 0]
        half_seq = [0]
        prev_silu = [None]
        prev_silu_tile = [None]
        scr_dma_tok = {}
        pending_combines = [None]

        @block.vector
        def _(vector):
            tensor = nc.tensor
            scalar = Chained(nc.scalar, "sc", "sc", c)
            vector = Chained(nc.vector, "ve", "ve", c)
            gpsimd = Chained(nc.gpsimd, "gp", "gpc", c)
            sync = nc.sync

            dma_seq = [0]
            split_seq = [0]

            def pump_dma():
                hs = dma_seq[0]
                if hs >= nt * NH:
                    return
                dma_seq[0] += 1
                q = hs % 2
                HF = 4 * HW
                if hs >= NH:
                    return
                dram, col = half_list[hs % NH]
                rr = dram.rearrange("(k p) c -> p k c", p=P)
                for tok in w32_free[q]:
                    c.needt(sync, "sync", tok)
                for pc in range(2):
                    wdn = "wd" + str(q) + ("a" if pc == 0 else "b")
                    src3 = rr[:, 4 * pc:4 * pc + 4, col:col + HW]
                    ins = sync.dma_start(
                        out=w32[q][:, pc * HF:(pc + 1) * HF].rearrange(
                            "p (k c) -> p k c", c=HW),
                        in_=src3)
                    c.inc(ins, wdn, 16)

            def pump_split():
                hs = split_seq[0]
                if hs >= nt * NH:
                    return
                assert hs < dma_seq[0]
                split_seq[0] += 1
                par = hs % 3
                q = hs % 2
                HF = 4 * HW
                if hs >= NH:
                    hrow = hs % NH
                    wdn = "wds" + str(par)
                    c.needt(sync, "sync", whl_free[par])
                    c.need(sync, "sync", "gw" + str(hrow % 3),
                           c.cnt["gw" + str(hrow % 3)])
                    for pc, buf in ((0, wh), (1, wl)):
                        ins = sync.dma_start(
                            out=buf[par][:, :],
                            in_=wscr[hrow * P:(hrow + 1) * P,
                                     pc * 2048:(pc + 1) * 2048])
                        c.inc(ins, wdn, 16)
                    w_ready[par] = [(wdn, c.cnt[wdn])]
                    return
                lo_toks = []
                for pc in range(2):
                    wdn = "wd" + str(q) + ("a" if pc == 0 else "b")
                    c.need(scalar, "sc", wdn, c.cnt[wdn])
                    c.needt(scalar, "sc", whl_free[par])
                    c.need(scalar, "sc", "gw" + str(par), c.cnt["gw" + str(par)])
                    ins = scalar.activation(
                        wh[par][:, pc * HF:(pc + 1) * HF],
                        w32[q][:, pc * HF:(pc + 1) * HF], AF.Copy)
                    hi = ("sc", c.inc(ins, "sc"))
                    c.needt(gpsimd, "gp", hi)
                    c.needt(gpsimd, "gp", whl_free[par])
                    c.need(gpsimd, "gp", wdn, c.cnt[wdn])
                    ins = gpsimd.tensor_tensor(
                        wl[par][:, pc * HF:(pc + 1) * HF],
                        w32[q][:, pc * HF:(pc + 1) * HF],
                        wh[par][:, pc * HF:(pc + 1) * HF], OP.subtract)
                    lo_toks.append(("gpc", c.inc(ins, "gpc")))
                w32_free[q] = lo_toks
                w_ready[par] = lo_toks
                gwn = "gw" + str(par)
                for pc, buf in ((0, wh), (1, wl)):
                    ins = gpsimd.dma_start(
                        out=wscr[hs * P:(hs + 1) * P, pc * 2048:(pc + 1) * 2048],
                        in_=buf[par][:, :])
                    c.inc(ins, gwn, 16)

            def load_next_half():
                pump_split()
                pump_dma()

            rot = [0]

            def next_bank():
                b = rot[0] % 8
                rot[0] += 1
                return b

            def mm_group(bidx, par, col, rh, rl, extra=None):
                bank = pg[bidx]
                c.needt(tensor, "pe", bank_free[bidx])
                for tok in w_ready[par]:
                    c.needt(tensor, "pe", tok)
                if extra:
                    for tok in extra:
                        c.needt(tensor, "pe", tok)
                last = None
                idx = 0
                for k in range(KC):
                    wcol = k * HW + col
                    for wop, xop in ((wh[par], rh), (wh[par], rl),
                                     (wl[par], rh)):
                        last = tensor.matmul(
                            bank[:, :], wop[:, wcol:wcol + P],
                            xop[:, k * TT:(k + 1) * TT],
                            start=(idx == 0), stop=(idx == 23))
                        idx += 1
                return ("pe", c.inc(last, "pe"))

            for apx, val in [(ones_c[:, :], 1.0), (ones_r[:, :], 1.0),
                             (hcarry[:, :], 0.0)]:
                c.inc(gpsimd.memset(apx, val), "gpc")
            for k in range(KC):
                c.inc(gpsimd.memset(xa[:, k * XAW:k * XAW + 3], 0.0), "gpc")
            tok_init = ("gpc", c.cnt["gpc"])
            ins = gpsimd.dma_start(out=vecs[:, :], in_=vecs_d[:, :])
            vecs_tok = ("gv", c.inc(ins, "gv", 16))
            c.needt(scalar, "sc", vecs_tok)
            c.inc(scalar.activation(ce[:, :], vecs[:, C_FB:C_FB + KC], AF.Exp), "sc")
            c.inc(scalar.activation(ce[:, :], ce[:, :], AF.Ln, bias=1.0), "sc")
            c.inc(scalar.activation(ce2[:, :], ce[:, :], AF.Copy,
                                    scale=vecs[:, C_M16:C_M16 + 1]), "sc")
            ins = scalar.activation(ce[:, :], ce[:, :], AF.Copy,
                                    scale=vecs[:, C_M8:C_M8 + 1])
            ce_tok = ("sc", c.inc(ins, "sc"))
            c.needt(vector, "ve", vecs_tok)
            c.needt(vector, "ve", tok_init)
            c.needt(tensor, "pe", tok_init)

            pump_dma()
            pump_dma()
            pump_split()
            pump_split()
            pump_dma()
            pump_split()
            pump_dma()

            def emit_xload(t, k):
                t0 = t * TT
                c.needt(gpsimd, "gp", free_hb[k])
                ins = gpsimd.dma_start(
                    out=hb[:, k * TT:(k + 1) * TT],
                    in_=xT[k * P:(k + 1) * P, t0:t0 + TT])
                c.inc(ins, "gx", 16)

            def emit_LN(t, preloaded=False):
                t0 = t * TT
                if not preloaded:
                    for k in range(KC):
                        emit_xload(t, k)
                xtok = ("gx", c.cnt["gx"])
                bsum, bsq = next_bank(), next_bank()
                sum_tok = sq_tok = None
                for k in range(KC):
                    sl = k % 2
                    c.needt(gpsimd, "gp", xtok)
                    c.needt(gpsimd, "gp", free_A[sl])
                    ins = gpsimd.tensor_tensor(
                        poolA[:, sl * TT:(sl + 1) * TT],
                        hb[:, k * TT:(k + 1) * TT],
                        hb[:, k * TT:(k + 1) * TT], OP.mult)
                    sq_rdy = ("gpc", c.inc(ins, "gpc"))
                    c.needt(tensor, "pe", xtok)
                    if k == 0:
                        c.needt(tensor, "pe", bank_free[bsum])
                        c.needt(tensor, "pe", bank_free[bsq])
                    ins = tensor.matmul(pg[bsum][0:1, :], ones_r[:, :],
                                        hb[:, k * TT:(k + 1) * TT],
                                        start=(k == 0), stop=(k == KC - 1))
                    if k == KC - 1:
                        sum_tok = ("pe", c.inc(ins, "pe"))
                    c.needt(tensor, "pe", sq_rdy)
                    ins = tensor.matmul(pg[bsq][0:1, :], ones_r[:, :],
                                        poolA[:, sl * TT:(sl + 1) * TT],
                                        start=(k == 0), stop=(k == KC - 1))
                    free_A[sl] = sq_tok = ("pe", c.inc(ins, "pe"))
                R = lambda j: rows[:, j * TT:(j + 1) * TT]
                c.needt(scalar, "sc", sum_tok)
                c.needt(scalar, "sc", sq_tok)
                c.needt(scalar, "sc", free_rows[0])
                ins = scalar.activation(R(0), pg[bsum][0:1, :], AF.Copy,
                                        scale=vecs[0:1, C_INVD:C_INVD + 1])
                bank_free[bsum] = ("sc", c.inc(ins, "sc"))
                ins = scalar.activation(R(1), pg[bsq][0:1, :], AF.Copy,
                                        scale=vecs[0:1, C_INVD:C_INVD + 1])
                bank_free[bsq] = ("sc", c.inc(ins, "sc"))
                ins = scalar.activation(R(2), R(0), AF.Square)
                m2_tok = ("sc", c.inc(ins, "sc"))
                c.needt(vector, "ve", m2_tok)
                ins = vector.tensor_tensor(R(1), R(1), R(2), OP.subtract)
                var_tok = ("ve", c.inc(ins, "ve"))
                c.needt(scalar, "sc", var_tok)
                ins = scalar.activation(R(3), R(1), AF.Ln,
                                        bias=vecs[0:1, C_EPS:C_EPS + 1])
                c.inc(ins, "sc")
                ins = scalar.activation(R(2), R(3), AF.Exp,
                                        scale=vecs[0:1, C_MH:C_MH + 1])
                rstd_tok = ("sc", c.inc(ins, "sc"))
                c.needt(vector, "ve", rstd_tok)
                ins = vector.tensor_tensor(R(0), R(0), R(2), OP.mult)
                c.inc(ins, "ve")
                c.need(scalar, "sc", "ve", c.cnt["ve"])
                ins = scalar.activation(R(3), R(0), AF.Copy,
                                        scale=vecs[0:1, C_M1:C_M1 + 1])
                brow_tok = ("sc", c.inc(ins, "sc"))
                return (rstd_tok, brow_tok)

            def emit_LN_post(t):
                rstd_tok, brow_tok = ln_rows_box[0]
                R = lambda j: rows[:, j * TT:(j + 1) * TT]
                bA, bB = next_bank(), next_bank()
                c.needt(tensor, "pe", rstd_tok)
                c.needt(tensor, "pe", brow_tok)
                c.needt(tensor, "pe", bank_free[bA])
                c.needt(tensor, "pe", bank_free[bB])
                ins = tensor.matmul(pg[bA][:, :], ones_c[:, :], R(2),
                                    start=True, stop=True)
                c.inc(ins, "pe")
                ins = tensor.matmul(pg[bB][:, :], ones_c[:, :], R(3),
                                    start=True, stop=True)
                bcB_tok = ("pe", c.inc(ins, "pe"))
                free_rows[0] = bcB_tok
                xn_last = None
                for k in range(KC):
                    sl = k % 2
                    c.needt(vector, "ve", bcB_tok)
                    c.needt(vector, "ve", free_xn[0])
                    c.needt(vector, "ve", free_B[sl])
                    PB = poolB[:, sl * TT:(sl + 1) * TT]
                    ins = vector.tensor_tensor(
                        PB, hb[:, k * TT:(k + 1) * TT], pg[bA][:, :], OP.mult)
                    c.inc(ins, "ve")
                    ins = vector.tensor_tensor(PB, PB, pg[bB][:, :], OP.add)
                    c.inc(ins, "ve")
                    ins = vector.tensor_copy(xnh[:, k * TT:(k + 1) * TT], PB)
                    c.inc(ins, "ve")
                    ins = vector.tensor_sub(
                        xnl[:, k * TT:(k + 1) * TT], PB,
                        xnh[:, k * TT:(k + 1) * TT])
                    xn_last = ("ve", c.inc(ins, "ve"))
                    free_B[sl] = xn_last
                bank_free[bA] = bank_free[bB] = xn_last
                return xn_last

            ln_rows_box = [None]
            ln_rows_box[0] = emit_LN(0)
            xn_rdy_box = [None]
            xn_rdy_box[0] = emit_LN_post(0)

            for t in range(nt):
                t0 = t * TT
                xn_rdy = xn_rdy_box[0]

                for j, m in enumerate(list(range(8, 16)) + list(range(8))):
                    par = (t * NH + j // 2) % 3
                    bidx = next_bank()
                    gtok = mm_group(bidx, par, (j % 2) * P, xnh, xnl,
                                    extra=[xn_rdy])
                    if j % 2 == 1:
                        whl_free[par] = gtok
                        load_next_half()
                    if m < 8:
                        c.needt(scalar, "sc", gtok)
                        c.needt(scalar, "sc", free_gg[m])
                        ins = scalar.activation(
                            gg[:, m * TT:(m + 1) * TT], pg[bidx][:, :], AF.Gelu)
                        bank_free[bidx] = ("sc", c.inc(ins, "sc"))
                    else:
                        k = m - 8
                        tail_tok = tok_init
                        if t > 0:
                            c.needt(vector, "ve", free_xa[k])
                            ins = vector.tensor_copy(
                                xa[:, k * XAW:k * XAW + 3],
                                xa[:, k * XAW + TT:k * XAW + TT + 3])
                            tail_tok = ("ve", c.inc(ins, "ve"))
                        c.needt(scalar, "sc", gtok)
                        c.needt(scalar, "sc", tail_tok)
                        ins = scalar.activation(
                            xa[:, k * XAW + 3:k * XAW + 3 + TT],
                            pg[bidx][:, :], AF.Copy)
                        ev_tok = ("sc", c.inc(ins, "sc"))
                        bank_free[bidx] = ev_tok
                        sl = k % 2
                        c.needt(vector, "ve", ev_tok)
                        if free_C[sl] is not None and free_C[sl][0] == "go":
                            c.need(vector, "ve", "go", c.cnt["go"])
                        else:
                            c.needt(vector, "ve", free_C[sl])
                        PC = poolC[:, sl * TT:(sl + 1) * TT]
                        ins = vector.tensor_scalar(
                            PC, xa[:, k * XAW + 3:k * XAW + 3 + TT],
                            vecs[:, C_WCONV + k * 4 + 3:C_WCONV + k * 4 + 4],
                            vecs[:, C_BCONV + k:C_BCONV + k + 1],
                            OP.mult, OP.add)
                        c.inc(ins, "ve")
                        for j in (2, 1, 0):
                            ins = vector.scalar_tensor_tensor(
                                PC, xa[:, k * XAW + j:k * XAW + j + TT],
                                vecs[:, C_WCONV + k * 4 + j:C_WCONV + k * 4 + j + 1],
                                PC, OP.mult, OP.add)
                            tok = ("ve", c.inc(ins, "ve"))
                        free_xa[k] = tok
                        c.needt(vector, "ve", free_y_pe[0])
                        ins = vector.tensor_copy(
                            xch[:, k * TT:(k + 1) * TT], PC)
                        c.inc(ins, "ve")
                        ins = vector.tensor_sub(
                            xcl[:, k * TT:(k + 1) * TT], PC,
                            xch[:, k * TT:(k + 1) * TT])
                        xc_rdy = ("ve", c.inc(ins, "ve"))
                        free_C[sl] = xc_rdy
                        if pending_combines[0] is not None:
                            pending_combines[0](k)
                            if k == 7:
                                pending_combines[0] = None
                c.needt(scalar, "sc", ce_tok)
                scan_toks = [None] * KC
                y_toks = [None] * KC

                def emit_EL_a(k):
                    sl = k % 2
                    SFk = sf[:, k * TT:(k + 1) * TT]
                    SIk = si[:, k * TT:(k + 1) * TT]
                    ALk = al[:, k * TT:(k + 1) * TT]
                    SC = scr[:, sl * TT:(sl + 1) * TT]
                    SC2 = scr2[:, sl * TT:(sl + 1) * TT]
                    SC3 = poolC[:, sl * TT:(sl + 1) * TT]
                    c.needt(scalar, "sc", free_al[k])
                    ins = scalar.activation(ALk, SFk, AF.Exp,
                                            scale=ce[:, k:k + 1])
                    al_tok = ("sc", c.inc(ins, "sc"))
                    c.needt(scalar, "sc", free_scr[sl])
                    ins = scalar.activation(SC, SFk, AF.Exp,
                                            scale=ce2[:, k:k + 1])
                    c.inc(ins, "sc")
                    c.need(vector, "ve", "sc", c.cnt["sc"])
                    ins = vector.tensor_scalar(SC, SC, -1.0, 1.0 + 1e-6,
                                               OP.mult, OP.add)
                    ztok = ("ve", c.inc(ins, "ve"))
                    c.needt(scalar, "sc", ztok)
                    ins = scalar.activation(SC, SC, AF.Ln)
                    c.inc(ins, "sc")
                    ins = scalar.activation(SFk, SC, AF.Exp,
                                            scale=vecs[:, C_H:C_H + 1])
                    beta_tok = ("sc", c.inc(ins, "sc"))
                    free_scr[sl] = beta_tok
                    c.needt(vector, "ve", beta_tok)
                    c.needt(vector, "ve", free_scr2[sl])
                    ins = vector.tensor_tensor(SC2, SFk, SIk, OP.mult)
                    ut_tok = ("ve", c.inc(ins, "ve"))
                    free_sf[k] = ut_tok
                    c.need(vector, "ve", "go", c.cnt["go"])
                    ins = vector.tensor_tensor(
                        SC3, xch[:, k * TT:(k + 1) * TT],
                        xcl[:, k * TT:(k + 1) * TT], OP.add)
                    c.inc(ins, "ve")
                    ins = vector.tensor_tensor(SIk, SC2, SC3, OP.mult)
                    u_tok = ("ve", c.inc(ins, "ve"))
                    free_scr2[sl] = u_tok
                    u_toks[k] = u_tok
                    c.needt(vector, "ve", al_tok)
                    HBk = hb[:, k * TT:(k + 1) * TT]
                    ins = vector.tensor_tensor_scan(
                        HBk, ALk, SIk, hcarry[:, k:k + 1], OP.mult, OP.add)
                    scan_toks[k] = ("ve", c.inc(ins, "ve"))
                    free_al[k] = scan_toks[k]
                    ins = vector.tensor_copy(
                        hcarry[:, k:k + 1], hb[:, k * TT + TT - 1:k * TT + TT])
                    c.inc(ins, "ve")

                def emit_EL_b(k):
                    sl = k % 2
                    HBk = hb[:, k * TT:(k + 1) * TT]
                    c.needt(vector, "ve", free_B[sl])
                    PB = poolB[:, sl * TT:(sl + 1) * TT]
                    ins = vector.tensor_tensor(
                        PB, gg[:, k * TT:(k + 1) * TT], HBk, OP.mult)
                    y32_tok = ("ve", c.inc(ins, "ve"))
                    free_hb[k] = y32_tok
                    c.needt(vector, "ve", free_xc_pe[0])
                    ins = vector.tensor_copy(xch[:, k * TT:(k + 1) * TT], PB)
                    c.inc(ins, "ve")
                    ins = vector.tensor_sub(
                        xcl[:, k * TT:(k + 1) * TT], PB,
                        xch[:, k * TT:(k + 1) * TT])
                    y_toks[k] = ("ve", c.inc(ins, "ve"))
                    free_B[sl] = y_toks[k]
                    if t + 1 < nt:
                        emit_xload(t + 1, k)

                g2_order = []
                for p2 in range(4):
                    g2_order += [2 * p2, 2 * p2 + 1, 8 + 2 * p2, 9 + 2 * p2]
                for j2, m in enumerate(g2_order):
                    par = (t * NH + 8 + j2 // 2) % 3
                    bidx = next_bank()
                    gtok = mm_group(bidx, par, (j2 % 2) * P, xch, xcl,
                                    extra=[xc_rdy])
                    if j2 % 2 == 1:
                        whl_free[par] = gtok
                        load_next_half()
                    if j2 == 15:
                        free_xc_pe[0] = gtok
                    c.needt(scalar, "sc", gtok)
                    if m < 8:
                        c.need(scalar, "sc", "ve", free_sf[m][1]) if free_sf[m] else None
                        ins = scalar.activation(
                            sf[:, m * TT:(m + 1) * TT], pg[bidx][:, :],
                            AF.Sigmoid, bias=vecs[:, C_BGF + m:C_BGF + m + 1])
                    else:
                        c.needt(scalar, "sc", free_si[0])
                        ins = scalar.activation(
                            si[:, (m - 8) * TT:(m - 7) * TT], pg[bidx][:, :],
                            AF.Sigmoid,
                            bias=vecs[:, C_BGI + (m - 8):C_BGI + (m - 8) + 1])
                    bank_free[bidx] = ("sc", c.inc(ins, "sc"))
                    if j2 % 4 == 3:
                        p2 = j2 // 4
                        emit_EL_a(2 * p2)
                        emit_EL_a(2 * p2 + 1)

                silu_toks = [None] * KC

                def emit_silu(ii):
                    c.needt(scalar, "sc", acc_toks[ii])
                    c.need(scalar, "sc", "ve", free_hb[ii][1])
                    ins = scalar.activation(
                        gg[:, ii * TT:(ii + 1) * TT], accb[:, :], AF.Silu)
                    silu_toks[ii] = ("sc", c.inc(ins, "sc"))

                acc_toks = [None] * KC
                for i in range(8):
                    L_toks = [None] * 8
                    C_toks = [None] * 8
                    for n in range(8):
                        par = (t * NH + 16 + 8 * i + n // 2) % 3
                        bidx = next_bank()
                        gtok = mm_group(bidx, par, (n % 2) * P, xnh, xnl)
                        if n % 2 == 1:
                            whl_free[par] = gtok
                            load_next_half()
                        if i == 0:
                            emit_EL_b(n)
                        c.needt(scalar, "sc", gtok)
                        if free_sf[n] is not None:
                            c.needt(scalar, "sc", free_sf[n])
                        ins = scalar.activation(
                            sf[:, n * TT:(n + 1) * TT], pg[bidx][:, :],
                            AF.Copy)
                        L_toks[n] = ("sc", c.inc(ins, "sc"))
                        bank_free[bidx] = L_toks[n]
                    for n in range(8):
                        par = (t * NH + 16 + 8 * i + 4 + n // 2) % 3
                        bidx = next_bank()
                        gtok = mm_group(bidx, par, (n % 2) * P, xnh, xnl)
                        if n % 2 == 1:
                            whl_free[par] = gtok
                            load_next_half()
                        if i == 7 and n == 7:
                            free_xn[0] = gtok
                        c.needt(vector, "ve", gtok)
                        ins = vector.tensor_scalar(
                            si[:, n * TT:(n + 1) * TT], pg[bidx][:, :],
                            vecs[:, C_CMPB + i * 8 + n:C_CMPB + i * 8 + n + 1],
                            None, OP.add)
                        C_toks[n] = ("ve", c.inc(ins, "ve"))
                        bank_free[bidx] = C_toks[n]
                    if i == 6 and t + 1 < nt:
                        ln_rows_box[0] = emit_LN(t + 1, preloaded=True)
                    if i > 0:
                        emit_silu(i - 1)
                    c.needt(vector, "ve", L_toks[7])
                    ins = vector.tensor_scalar(
                        mxb[:, :], sf[:, 0:TT],
                        vecs[:, C_SELB + i * 8:C_SELB + i * 8 + 1],
                        None, OP.add)
                    c.inc(ins, "ve")
                    for n in range(1, 8):
                        ins = vector.scalar_tensor_tensor(
                            mxb[:, :], sf[:, n * TT:(n + 1) * TT],
                            vecs[:, C_SELB + i * 8 + n:C_SELB + i * 8 + n + 1],
                            mxb[:, :], OP.add, OP.max)
                        c.inc(ins, "ve")
                    for n in range(8):
                        ins = vector.scalar_tensor_tensor(
                            msk[:, n * TT:(n + 1) * TT],
                            sf[:, n * TT:(n + 1) * TT],
                            vecs[:, C_SELB + i * 8 + n:C_SELB + i * 8 + n + 1],
                            mxb[:, :], OP.add, OP.is_equal)
                        tok = ("ve", c.inc(ins, "ve"))
                        free_sf[n] = tok
                    acc_guard = silu_toks[i - 1] if i > 0 else prev_silu_tile[0]
                    if acc_guard is not None:
                        c.needt(vector, "ve", acc_guard)
                    ins = vector.tensor_copy(accb[:, :], si[:, 7 * TT:8 * TT])
                    c.inc(ins, "ve")
                    for n in range(6, -1, -1):
                        ins = vector.copy_predicated(
                            accb[:, :], msk[:, n * TT:(n + 1) * TT],
                            si[:, n * TT:(n + 1) * TT])
                        tok = ("ve", c.inc(ins, "ve"))
                    free_si[0] = tok
                    acc_toks[i] = ("ve", tok[1])
                    if i == 7:
                        emit_silu(7)
                        prev_silu_tile[0] = silu_toks[7]

                y_rdy = y_toks[KC - 1]

                s_toks = [None] * KC
                combine_done = [False] * 8

                def emit_combine(i, t0=t0, s_toks=s_toks,
                                 silu_toks=silu_toks):
                    combine_done[i] = True
                    xsl = i % 2
                    c.needt(sync, "sync", free_A[xsl])
                    ins = sync.dma_start(
                        out=poolA[:, xsl * TT:(xsl + 1) * TT],
                        in_=xT[i * P:(i + 1) * P, t0:t0 + TT])
                    xr_tok = ("gr", c.inc(ins, "gr", 16))
                    osl = i % 2
                    if t > 0 or i > 1:
                        c.need(vector, "ve", "go", c.cnt["go"])
                    c.needt(vector, "ve", silu_toks[i])
                    c.needt(vector, "ve", s_toks[i])
                    OBi = poolC[:, osl * TT:(osl + 1) * TT]
                    ins = vector.tensor_tensor(
                        OBi, al[:, i * TT:(i + 1) * TT],
                        gg[:, i * TT:(i + 1) * TT], OP.mult)
                    tok = ("ve", c.inc(ins, "ve"))
                    free_al[i] = tok
                    free_gg[i] = tok
                    c.needt(vector, "ve", xr_tok)
                    c.need(vector, "ve", "gr", c.cnt["gr"])
                    ins = vector.tensor_tensor(
                        OBi, OBi, poolA[:, xsl * TT:(xsl + 1) * TT], OP.add)
                    otok = ("ve", c.inc(ins, "ve"))
                    free_A[xsl] = otok
                    c.need(sync, "sync", "ve", otok[1])
                    ins = sync.dma_start(
                        out=outT[i * P:(i + 1) * P, t0:t0 + TT], in_=OBi)
                    c.inc(ins, "go", 16)
                    free_C[osl] = ("go", c.cnt["go"])

                for m in range(8):
                    par = (t * NH + 80 + m // 2) % 3
                    bidx = next_bank()
                    gtok = mm_group(bidx, par, (m % 2) * P, xch, xcl,
                                    extra=[y_rdy])
                    if m % 2 == 1:
                        whl_free[par] = gtok
                        load_next_half()
                    if m == 1 and t + 1 < nt:
                        xn_rdy_box[0] = emit_LN_post(t + 1)
                    if m == 7:
                        free_y_pe[0] = gtok
                    c.needt(scalar, "sc", gtok)
                    c.need(scalar, "sc", "ve", scan_toks[m][1])
                    ins = scalar.activation(
                        al[:, m * TT:(m + 1) * TT], pg[bidx][:, :], AF.Copy)
                    s_toks[m] = ("sc", c.inc(ins, "sc"))
                    bank_free[bidx] = s_toks[m]
                    if t == nt - 1:
                        emit_combine(m)

                if t < nt - 1:
                    pending_combines[0] = emit_combine

            gpsimd.wait_ge(c.h["go"], c.cnt["go"])
            gpsimd.wait_ge(c.h["gx"], c.cnt["gx"])
            gpsimd.wait_ge(c.h["gr"], c.cnt["gr"])
            gpsimd.wait_ge(c.h["gv"], c.cnt["gv"])
            for sg in ("gw0", "gw1", "gw2"):
                gpsimd.wait_ge(c.h[sg], c.cnt[sg])
            for s4 in ("wd0a", "wd0b", "wd1a", "wd1b", "wds0", "wds1", "wds2"):
                sync.wait_ge(c.h[s4], c.cnt[s4])

    return nc


def _prep(L0_attn_x, L0_attn_w_in, L0_attn_w_conv, L0_attn_b_conv,
          L0_attn_w_gates, L0_attn_b_gates, L0_attn_forget_base,
          L0_attn_w_out, L0_ffn_sel_w, L0_ffn_sel_b, L0_ffn_comp_w,
          L0_ffn_comp_b):
    A = lambda v: np.ascontiguousarray(np.asarray(v, dtype=np.float32))
    x = A(L0_attn_x)
    xT = np.ascontiguousarray(x.transpose(0, 2, 1))
    w1T = A(np.asarray(L0_attn_w_in).T)
    w2T = A(np.asarray(L0_attn_w_gates).T)
    w3T = A(np.asarray(L0_attn_w_out).T)
    sel = A(L0_ffn_sel_w)
    selT = sel.T.reshape(D, 1024, 8)
    selT = selT.reshape(D, 8, 128, 8)
    selT = np.ascontiguousarray(selT.transpose(0, 1, 3, 2).reshape(D, 8192))
    cw = A(L0_ffn_comp_w)
    cmpT = cw.transpose(2, 1, 0)
    cmpT = cmpT.reshape(D, 8, 128, 8)
    cmpT = np.ascontiguousarray(cmpT.transpose(0, 1, 3, 2).reshape(D, 8192))
    vecs = np.zeros((P, VW), dtype=np.float32)
    vecs[:, C_FB:C_FB + KC] = A(L0_attn_forget_base).reshape(KC, P).T
    vecs[:, C_BCONV:C_BCONV + KC] = A(L0_attn_b_conv).reshape(KC, P).T
    wc = A(L0_attn_w_conv).reshape(D, 4).reshape(KC, P, 4)
    for k in range(KC):
        vecs[:, C_WCONV + k * 4:C_WCONV + (k + 1) * 4] = wc[k]
    bg = A(L0_attn_b_gates)
    vecs[:, C_BGF:C_BGF + KC] = bg[:D].reshape(KC, P).T
    vecs[:, C_BGI:C_BGI + KC] = bg[D:].reshape(KC, P).T
    sbr = A(L0_ffn_sel_b).reshape(1024, 8).reshape(KC, P, 8)
    cbr = A(L0_ffn_comp_b).T.reshape(KC, P, 8)
    for i in range(KC):
        for n in range(8):
            vecs[:, C_SELB + i * 8 + n] = sbr[i, :, n]
            vecs[:, C_CMPB + i * 8 + n] = cbr[i, :, n]
    for col, val in ((C_M8, -8.0), (C_M16, -16.0), (C_INVD, 1.0 / D),
                     (C_EPS, EPS), (C_MH, -0.5), (C_M1, -1.0), (C_H, 0.5)):
        vecs[:, col] = val
    return xT, w1T, w2T, w3T, selT, cmpT, vecs


def kernel(**inputs):
    xT, w1T, w2T, w3T, selT, cmpT, vecs = _prep(**inputs)
    nc = build(nt=4)
    in_maps = []
    for cid in range(8):
        in_maps.append({
            "xT": xT[cid], "w1T": w1T, "w2T": w2T, "w3T": w3T,
            "selT": selT, "cmpT": cmpT, "vecs": vecs,
        })
    res = run_bass_kernel_spmd(nc, in_maps, core_ids=list(range(8)))
    out = np.stack([np.asarray(res.results[cid]["outT"]).T
                    for cid in range(8)])
    return np.ascontiguousarray(out.astype(np.float32))


# version 22
# speedup vs baseline: 1.1397x; 1.1397x over previous
import numpy as np
import concourse.bass as bass
import concourse.mybir as mybir
from concourse.bass_utils import run_bass_kernel_spmd

F32 = mybir.dt.float32
BF16 = mybir.dt.bfloat16
U8 = mybir.dt.uint8
AF = mybir.ActivationFunctionType
OP = mybir.AluOpType

P = 128
D = 1024
KC = 8
T = 2048
TT = 512
HW = 256
EPS = 1e-5

C_FB = 0
C_BCONV = 8
C_WCONV = 16
C_BGF = 48
C_BGI = 56
C_SELB = 64
C_CMPB = 128
C_M8 = 192
C_M16 = 193
C_INVD = 194
C_EPS = 195
C_MH = 196
C_M1 = 197
C_H = 198
VW = 200


class Ctl:
    def __init__(self, nc, es):
        self.nc = nc
        self.h = {}
        self.cnt = {}
        self.waited = {}
        for name in ("wd0a", "wd0b", "wd1a", "wd1b",
                     "gw0", "gw1", "gw2", "wds0", "wds1", "wds2",
                     "gx", "gr", "go0", "go1", "gv",
                     "pe", "sc", "ve", "gpc"):
            self.h[name] = es.enter_context(nc.semaphore(name))
            self.cnt[name] = 0

    def need(self, eng, engname, sem, val):
        if val is None or val <= 0:
            return
        key = (engname, sem)
        if self.waited.get(key, 0) >= val:
            return
        assert val <= self.cnt[sem], (
            f"wait {sem}>={val} but only {self.cnt[sem]} incs emitted")
        eng.wait_ge(self.h[sem], val)
        self.waited[key] = val

    def needt(self, eng, engname, tok):
        if tok is not None:
            self.need(eng, engname, tok[0], tok[1])

    def inc(self, instr, sem, n=1):
        instr.then_inc(self.h[sem], n)
        self.cnt[sem] += n
        return self.cnt[sem]


class Chained:

    def __init__(self, eng, engname, sem, ctl):
        self._eng = eng
        self._engname = engname
        self._sem = sem
        self._ctl = ctl

    def __getattr__(self, name):
        fn = getattr(self._eng, name)
        if name in ("wait_ge", "nop"):
            return fn

        def wrapped(*a, **k):
            ctl = self._ctl
            ctl.need(self._eng, self._engname, self._sem, ctl.cnt[self._sem])
            return fn(*a, **k)
        return wrapped


def build(nt=4):
    nc = bass.Bass("TRN2", target_bir_lowering=False, debug=False)

    xT = nc.dram_tensor("xT", [D, T], F32, kind="ExternalInput").ap()
    w1T = nc.dram_tensor("w1T", [D, 2048], F32, kind="ExternalInput").ap()
    w2T = nc.dram_tensor("w2T", [D, 2048], F32, kind="ExternalInput").ap()
    w3T = nc.dram_tensor("w3T", [D, 1024], F32, kind="ExternalInput").ap()
    selT = nc.dram_tensor("selT", [D, 8192], F32, kind="ExternalInput").ap()
    cmpT = nc.dram_tensor("cmpT", [D, 8192], F32, kind="ExternalInput").ap()
    vecs_d = nc.dram_tensor("vecs", [P, VW], F32, kind="ExternalInput").ap()
    outT = nc.dram_tensor("outT", [D, T], F32, kind="ExternalOutput").ap()
    wscr = nc.dram_tensor("wscr", [84 * P, 2 * 8 * 256], BF16).ap()

    half_list = []
    for s in (2, 3, 0, 1):
        half_list.append((w1T, s * TT))
        half_list.append((w1T, s * TT + HW))
    for p2 in range(4):
        half_list.append((w2T, p2 * HW))
        half_list.append((w2T, 1024 + p2 * HW))
    for i in range(8):
        for s in (2 * i, 2 * i + 1):
            half_list.append((selT, s * TT))
            half_list.append((selT, s * TT + HW))
        for s in (2 * i, 2 * i + 1):
            half_list.append((cmpT, s * TT))
            half_list.append((cmpT, s * TT + HW))
    for s in range(2):
        half_list.append((w3T, s * TT))
        half_list.append((w3T, s * TT + HW))
    NH = len(half_list)

    from contextlib import ExitStack
    es = ExitStack()
    with es:
        sb = lambda n, s, d=F32: es.enter_context(nc.sbuf_tensor(n, s, d))
        w32 = [sb("w320", [P, 8 * HW]), sb("w321", [P, 8 * HW])]
        wh = [sb("wh0", [P, 8 * HW], BF16), sb("wh1", [P, 8 * HW], BF16),
              sb("wh2", [P, 8 * HW], BF16)]
        wl = [sb("wl0", [P, 8 * HW], BF16), sb("wl1", [P, 8 * HW], BF16),
              sb("wl2", [P, 8 * HW], BF16)]
        xnh = sb("xnh", [P, 8 * TT], BF16)
        xnl = sb("xnl", [P, 8 * TT], BF16)
        xch = sb("xch", [P, 8 * TT], BF16)
        xcl = sb("xcl", [P, 8 * TT], BF16)
        gg = sb("gg", [P, 8 * TT])
        sf = sb("sf", [P, 8 * TT])
        si = sb("si", [P, 8 * TT])
        al = sb("al", [P, 8 * TT])
        hb = sb("hb", [P, 8 * TT])
        XAW = 520
        xa = sb("xa", [P, 8 * XAW])
        vecs = sb("vecsb", [P, VW])
        ce = sb("ce", [P, KC])
        ce2 = sb("ce2", [P, KC])
        hcarry = sb("hcarry", [P, KC])
        rows = sb("rows", [1, 4 * TT])
        poolA = sb("poolA", [P, 2 * TT])
        poolB = sb("poolB", [P, 2 * TT])
        poolC = sb("poolC", [P, 2 * TT])
        scr = sb("scr", [P, 2 * TT])
        scr2 = sb("scr2", [P, 2 * TT])
        mxb = sb("mxb", [P, TT])
        accb = sb("accb", [P, TT])
        msk = sb("msk", [P, 8 * TT], U8)
        ones_c = sb("ones_c", [1, P])
        ones_r = sb("ones_r", [P, 1])

        pg = [es.enter_context(nc.psum_tensor(f"pg{i}", [P, TT], F32))
              for i in range(8)]

        c = Ctl(nc, es)
        block = es.enter_context(nc.Block())

        free_xn = [None]
        free_gg = [None] * KC
        free_hb = [None] * KC
        free_sf = [None] * KC
        free_si = [None]
        free_al = [None] * KC
        free_xa = [None] * KC
        free_xc_pe = [None]
        free_y_pe = [None]
        u_toks = [None] * KC
        free_A = [None, None]
        free_B = [None, None]
        free_C = [None, None]
        free_scr = [None, None]
        free_scr2 = [None, None]
        free_rows = [None]
        bank_free = [None] * 8
        whl_free = [None, None, None]
        w32_free = [[], []]
        w_ready = [[], [], []]
        wd_loaded = [0, 0]
        half_seq = [0]
        prev_silu = [None]
        prev_silu_tile = [None]
        scr_dma_tok = {}
        pending_combines = [None]

        @block.vector
        def _(vector):
            tensor = nc.tensor
            scalar = Chained(nc.scalar, "sc", "sc", c)
            vector = Chained(nc.vector, "ve", "ve", c)
            gpsimd = Chained(nc.gpsimd, "gp", "gpc", c)
            sync = nc.sync

            dma_seq = [0]
            split_seq = [0]

            def pump_dma():
                hs = dma_seq[0]
                if hs >= nt * NH:
                    return
                dma_seq[0] += 1
                q = hs % 2
                HF = 4 * HW
                if hs >= NH:
                    return
                dram, col = half_list[hs % NH]
                rr = dram.rearrange("(k p) c -> p k c", p=P)
                for tok in w32_free[q]:
                    c.needt(sync, "sync", tok)
                for pc in range(2):
                    wdn = "wd" + str(q) + ("a" if pc == 0 else "b")
                    src3 = rr[:, 4 * pc:4 * pc + 4, col:col + HW]
                    ins = sync.dma_start(
                        out=w32[q][:, pc * HF:(pc + 1) * HF].rearrange(
                            "p (k c) -> p k c", c=HW),
                        in_=src3)
                    c.inc(ins, wdn, 16)

            def pump_split():
                hs = split_seq[0]
                if hs >= nt * NH:
                    return
                assert hs < dma_seq[0]
                split_seq[0] += 1
                par = hs % 3
                q = hs % 2
                HF = 4 * HW
                if hs >= NH:
                    hrow = hs % NH
                    wdn = "wds" + str(par)
                    c.needt(sync, "sync", whl_free[par])
                    c.need(sync, "sync", "gw" + str(hrow % 3),
                           c.cnt["gw" + str(hrow % 3)])
                    for pc, buf in ((0, wh), (1, wl)):
                        ins = sync.dma_start(
                            out=buf[par][:, :],
                            in_=wscr[hrow * P:(hrow + 1) * P,
                                     pc * 2048:(pc + 1) * 2048])
                        c.inc(ins, wdn, 16)
                    w_ready[par] = [(wdn, c.cnt[wdn])]
                    return
                lo_toks = []
                for pc in range(2):
                    wdn = "wd" + str(q) + ("a" if pc == 0 else "b")
                    c.need(scalar, "sc", wdn, c.cnt[wdn])
                    c.needt(scalar, "sc", whl_free[par])
                    c.need(scalar, "sc", "gw" + str(par), c.cnt["gw" + str(par)])
                    ins = scalar.activation(
                        wh[par][:, pc * HF:(pc + 1) * HF],
                        w32[q][:, pc * HF:(pc + 1) * HF], AF.Copy)
                    hi = ("sc", c.inc(ins, "sc"))
                    c.needt(gpsimd, "gp", hi)
                    c.needt(gpsimd, "gp", whl_free[par])
                    c.need(gpsimd, "gp", wdn, c.cnt[wdn])
                    ins = gpsimd.tensor_tensor(
                        wl[par][:, pc * HF:(pc + 1) * HF],
                        w32[q][:, pc * HF:(pc + 1) * HF],
                        wh[par][:, pc * HF:(pc + 1) * HF], OP.subtract)
                    lo_toks.append(("gpc", c.inc(ins, "gpc")))
                w32_free[q] = lo_toks
                w_ready[par] = lo_toks
                gwn = "gw" + str(par)
                for pc, buf in ((0, wh), (1, wl)):
                    ins = gpsimd.dma_start(
                        out=wscr[hs * P:(hs + 1) * P, pc * 2048:(pc + 1) * 2048],
                        in_=buf[par][:, :])
                    c.inc(ins, gwn, 16)

            def load_next_half():
                pump_split()
                pump_dma()

            rot = [0]

            def next_bank():
                b = rot[0] % 8
                rot[0] += 1
                return b

            def mm_group(bidx, par, col, rh, rl, extra=None):
                bank = pg[bidx]
                c.needt(tensor, "pe", bank_free[bidx])
                for tok in w_ready[par]:
                    c.needt(tensor, "pe", tok)
                if extra:
                    for tok in extra:
                        c.needt(tensor, "pe", tok)
                last = None
                idx = 0
                for k in range(KC):
                    wcol = k * HW + col
                    for wop, xop in ((wh[par], rh), (wh[par], rl),
                                     (wl[par], rh)):
                        last = tensor.matmul(
                            bank[:, :], wop[:, wcol:wcol + P],
                            xop[:, k * TT:(k + 1) * TT],
                            start=(idx == 0), stop=(idx == 23))
                        idx += 1
                return ("pe", c.inc(last, "pe"))

            for apx, val in [(ones_c[:, :], 1.0), (ones_r[:, :], 1.0),
                             (hcarry[:, :], 0.0)]:
                c.inc(gpsimd.memset(apx, val), "gpc")
            for k in range(KC):
                c.inc(gpsimd.memset(xa[:, k * XAW:k * XAW + 3], 0.0), "gpc")
            tok_init = ("gpc", c.cnt["gpc"])
            ins = gpsimd.dma_start(out=vecs[:, :], in_=vecs_d[:, :])
            vecs_tok = ("gv", c.inc(ins, "gv", 16))
            c.needt(scalar, "sc", vecs_tok)
            c.inc(scalar.activation(ce[:, :], vecs[:, C_FB:C_FB + KC], AF.Exp), "sc")
            c.inc(scalar.activation(ce[:, :], ce[:, :], AF.Ln, bias=1.0), "sc")
            c.inc(scalar.activation(ce2[:, :], ce[:, :], AF.Copy,
                                    scale=vecs[:, C_M16:C_M16 + 1]), "sc")
            ins = scalar.activation(ce[:, :], ce[:, :], AF.Copy,
                                    scale=vecs[:, C_M8:C_M8 + 1])
            ce_tok = ("sc", c.inc(ins, "sc"))
            c.needt(vector, "ve", vecs_tok)
            c.needt(vector, "ve", tok_init)
            c.needt(tensor, "pe", tok_init)

            pump_dma()
            pump_dma()
            pump_split()
            pump_split()
            pump_dma()
            pump_split()
            pump_dma()

            def emit_xload(t, k):
                t0 = t * TT
                c.needt(gpsimd, "gp", free_hb[k])
                ins = gpsimd.dma_start(
                    out=hb[:, k * TT:(k + 1) * TT],
                    in_=xT[k * P:(k + 1) * P, t0:t0 + TT])
                c.inc(ins, "gx", 16)

            def emit_LN(t, preloaded=False):
                t0 = t * TT
                if not preloaded:
                    for k in range(KC):
                        emit_xload(t, k)
                xtok = ("gx", c.cnt["gx"])
                bsum, bsq = next_bank(), next_bank()
                sum_tok = sq_tok = None
                for k in range(KC):
                    sl = k % 2
                    c.needt(gpsimd, "gp", xtok)
                    c.needt(gpsimd, "gp", free_A[sl])
                    ins = gpsimd.tensor_tensor(
                        poolA[:, sl * TT:(sl + 1) * TT],
                        hb[:, k * TT:(k + 1) * TT],
                        hb[:, k * TT:(k + 1) * TT], OP.mult)
                    sq_rdy = ("gpc", c.inc(ins, "gpc"))
                    c.needt(tensor, "pe", xtok)
                    if k == 0:
                        c.needt(tensor, "pe", bank_free[bsum])
                        c.needt(tensor, "pe", bank_free[bsq])
                    ins = tensor.matmul(pg[bsum][0:1, :], ones_r[:, :],
                                        hb[:, k * TT:(k + 1) * TT],
                                        start=(k == 0), stop=(k == KC - 1))
                    if k == KC - 1:
                        sum_tok = ("pe", c.inc(ins, "pe"))
                    c.needt(tensor, "pe", sq_rdy)
                    ins = tensor.matmul(pg[bsq][0:1, :], ones_r[:, :],
                                        poolA[:, sl * TT:(sl + 1) * TT],
                                        start=(k == 0), stop=(k == KC - 1))
                    free_A[sl] = sq_tok = ("pe", c.inc(ins, "pe"))
                R = lambda j: rows[:, j * TT:(j + 1) * TT]
                c.needt(scalar, "sc", sum_tok)
                c.needt(scalar, "sc", sq_tok)
                c.needt(scalar, "sc", free_rows[0])
                ins = scalar.activation(R(0), pg[bsum][0:1, :], AF.Copy,
                                        scale=vecs[0:1, C_INVD:C_INVD + 1])
                bank_free[bsum] = ("sc", c.inc(ins, "sc"))
                ins = scalar.activation(R(1), pg[bsq][0:1, :], AF.Copy,
                                        scale=vecs[0:1, C_INVD:C_INVD + 1])
                bank_free[bsq] = ("sc", c.inc(ins, "sc"))
                ins = scalar.activation(R(2), R(0), AF.Square)
                m2_tok = ("sc", c.inc(ins, "sc"))
                c.needt(vector, "ve", m2_tok)
                ins = vector.tensor_tensor(R(1), R(1), R(2), OP.subtract)
                var_tok = ("ve", c.inc(ins, "ve"))
                c.needt(scalar, "sc", var_tok)
                ins = scalar.activation(R(3), R(1), AF.Ln,
                                        bias=vecs[0:1, C_EPS:C_EPS + 1])
                c.inc(ins, "sc")
                ins = scalar.activation(R(2), R(3), AF.Exp,
                                        scale=vecs[0:1, C_MH:C_MH + 1])
                rstd_tok = ("sc", c.inc(ins, "sc"))
                c.needt(vector, "ve", rstd_tok)
                ins = vector.tensor_tensor(R(0), R(0), R(2), OP.mult)
                c.inc(ins, "ve")
                c.need(scalar, "sc", "ve", c.cnt["ve"])
                ins = scalar.activation(R(3), R(0), AF.Copy,
                                        scale=vecs[0:1, C_M1:C_M1 + 1])
                brow_tok = ("sc", c.inc(ins, "sc"))
                return (rstd_tok, brow_tok)

            def emit_LN_post(t):
                rstd_tok, brow_tok = ln_rows_box[0]
                R = lambda j: rows[:, j * TT:(j + 1) * TT]
                bA, bB = next_bank(), next_bank()
                c.needt(tensor, "pe", rstd_tok)
                c.needt(tensor, "pe", brow_tok)
                c.needt(tensor, "pe", bank_free[bA])
                c.needt(tensor, "pe", bank_free[bB])
                ins = tensor.matmul(pg[bA][:, :], ones_c[:, :], R(2),
                                    start=True, stop=True)
                c.inc(ins, "pe")
                ins = tensor.matmul(pg[bB][:, :], ones_c[:, :], R(3),
                                    start=True, stop=True)
                bcB_tok = ("pe", c.inc(ins, "pe"))
                free_rows[0] = bcB_tok
                xn_last = None
                for k in range(KC):
                    sl = k % 2
                    c.needt(vector, "ve", bcB_tok)
                    c.needt(vector, "ve", free_xn[0])
                    c.needt(vector, "ve", free_B[sl])
                    PB = poolB[:, sl * TT:(sl + 1) * TT]
                    ins = vector.tensor_tensor(
                        PB, hb[:, k * TT:(k + 1) * TT], pg[bA][:, :], OP.mult)
                    c.inc(ins, "ve")
                    ins = vector.tensor_tensor(PB, PB, pg[bB][:, :], OP.add)
                    c.inc(ins, "ve")
                    ins = vector.tensor_copy(xnh[:, k * TT:(k + 1) * TT], PB)
                    c.inc(ins, "ve")
                    ins = vector.tensor_sub(
                        xnl[:, k * TT:(k + 1) * TT], PB,
                        xnh[:, k * TT:(k + 1) * TT])
                    xn_last = ("ve", c.inc(ins, "ve"))
                    free_B[sl] = xn_last
                bank_free[bA] = bank_free[bB] = xn_last
                return xn_last

            ln_rows_box = [None]
            ln_rows_box[0] = emit_LN(0)
            xn_rdy_box = [None]
            xn_rdy_box[0] = emit_LN_post(0)

            for t in range(nt):
                t0 = t * TT
                xn_rdy = xn_rdy_box[0]

                for j, m in enumerate(list(range(8, 16)) + list(range(8))):
                    par = (t * NH + j // 2) % 3
                    bidx = next_bank()
                    gtok = mm_group(bidx, par, (j % 2) * P, xnh, xnl,
                                    extra=[xn_rdy])
                    if j % 2 == 1:
                        whl_free[par] = gtok
                        load_next_half()
                    if m < 8:
                        c.needt(scalar, "sc", gtok)
                        c.needt(scalar, "sc", free_gg[m])
                        ins = scalar.activation(
                            gg[:, m * TT:(m + 1) * TT], pg[bidx][:, :], AF.Gelu)
                        bank_free[bidx] = ("sc", c.inc(ins, "sc"))
                    else:
                        k = m - 8
                        tail_tok = tok_init
                        if t > 0:
                            c.needt(vector, "ve", free_xa[k])
                            ins = vector.tensor_copy(
                                xa[:, k * XAW:k * XAW + 3],
                                xa[:, k * XAW + TT:k * XAW + TT + 3])
                            tail_tok = ("ve", c.inc(ins, "ve"))
                        c.needt(scalar, "sc", gtok)
                        c.needt(scalar, "sc", tail_tok)
                        ins = scalar.activation(
                            xa[:, k * XAW + 3:k * XAW + 3 + TT],
                            pg[bidx][:, :], AF.Copy)
                        ev_tok = ("sc", c.inc(ins, "sc"))
                        bank_free[bidx] = ev_tok
                        sl = k % 2
                        c.needt(vector, "ve", ev_tok)
                        if free_C[sl] is not None and free_C[sl][0].startswith("go"):
                            c.need(vector, "ve", free_C[sl][0],
                                   c.cnt[free_C[sl][0]])
                        else:
                            c.needt(vector, "ve", free_C[sl])
                        PC = poolC[:, sl * TT:(sl + 1) * TT]
                        ins = vector.tensor_scalar(
                            PC, xa[:, k * XAW + 3:k * XAW + 3 + TT],
                            vecs[:, C_WCONV + k * 4 + 3:C_WCONV + k * 4 + 4],
                            vecs[:, C_BCONV + k:C_BCONV + k + 1],
                            OP.mult, OP.add)
                        c.inc(ins, "ve")
                        for j in (2, 1, 0):
                            ins = vector.scalar_tensor_tensor(
                                PC, xa[:, k * XAW + j:k * XAW + j + TT],
                                vecs[:, C_WCONV + k * 4 + j:C_WCONV + k * 4 + j + 1],
                                PC, OP.mult, OP.add)
                            tok = ("ve", c.inc(ins, "ve"))
                        free_xa[k] = tok
                        c.needt(vector, "ve", free_y_pe[0])
                        ins = vector.tensor_copy(
                            xch[:, k * TT:(k + 1) * TT], PC)
                        c.inc(ins, "ve")
                        ins = vector.tensor_sub(
                            xcl[:, k * TT:(k + 1) * TT], PC,
                            xch[:, k * TT:(k + 1) * TT])
                        xc_rdy = ("ve", c.inc(ins, "ve"))
                        free_C[sl] = xc_rdy
                        if pending_combines[0] is not None:
                            pending_combines[0](k)
                            if k == 7:
                                pending_combines[0] = None
                c.needt(scalar, "sc", ce_tok)
                scan_toks = [None] * KC
                y_toks = [None] * KC

                def emit_EL_a(k):
                    sl = k % 2
                    SFk = sf[:, k * TT:(k + 1) * TT]
                    SIk = si[:, k * TT:(k + 1) * TT]
                    ALk = al[:, k * TT:(k + 1) * TT]
                    SC = scr[:, sl * TT:(sl + 1) * TT]
                    SC2 = scr2[:, sl * TT:(sl + 1) * TT]
                    SC3 = poolC[:, sl * TT:(sl + 1) * TT]
                    c.needt(scalar, "sc", free_al[k])
                    ins = scalar.activation(ALk, SFk, AF.Exp,
                                            scale=ce[:, k:k + 1])
                    al_tok = ("sc", c.inc(ins, "sc"))
                    c.needt(scalar, "sc", free_scr[sl])
                    ins = scalar.activation(SC, SFk, AF.Exp,
                                            scale=ce2[:, k:k + 1])
                    c.inc(ins, "sc")
                    c.need(vector, "ve", "sc", c.cnt["sc"])
                    ins = vector.tensor_scalar(SC, SC, -1.0, 1.0 + 1e-6,
                                               OP.mult, OP.add)
                    ztok = ("ve", c.inc(ins, "ve"))
                    c.needt(scalar, "sc", ztok)
                    ins = scalar.activation(SC, SC, AF.Ln)
                    c.inc(ins, "sc")
                    ins = scalar.activation(SFk, SC, AF.Exp,
                                            scale=vecs[:, C_H:C_H + 1])
                    beta_tok = ("sc", c.inc(ins, "sc"))
                    free_scr[sl] = beta_tok
                    c.needt(vector, "ve", beta_tok)
                    c.needt(vector, "ve", free_scr2[sl])
                    ins = vector.tensor_tensor(SC2, SFk, SIk, OP.mult)
                    ut_tok = ("ve", c.inc(ins, "ve"))
                    free_sf[k] = ut_tok
                    c.need(vector, "ve", "go0", c.cnt["go0"])
                    c.need(vector, "ve", "go1", c.cnt["go1"])
                    ins = vector.tensor_tensor(
                        SC3, xch[:, k * TT:(k + 1) * TT],
                        xcl[:, k * TT:(k + 1) * TT], OP.add)
                    c.inc(ins, "ve")
                    ins = vector.tensor_tensor(SIk, SC2, SC3, OP.mult)
                    u_tok = ("ve", c.inc(ins, "ve"))
                    free_scr2[sl] = u_tok
                    u_toks[k] = u_tok
                    c.needt(vector, "ve", al_tok)
                    HBk = hb[:, k * TT:(k + 1) * TT]
                    ins = vector.tensor_tensor_scan(
                        HBk, ALk, SIk, hcarry[:, k:k + 1], OP.mult, OP.add)
                    scan_toks[k] = ("ve", c.inc(ins, "ve"))
                    free_al[k] = scan_toks[k]
                    ins = vector.tensor_copy(
                        hcarry[:, k:k + 1], hb[:, k * TT + TT - 1:k * TT + TT])
                    c.inc(ins, "ve")

                def emit_EL_b(k):
                    sl = k % 2
                    HBk = hb[:, k * TT:(k + 1) * TT]
                    c.needt(vector, "ve", free_B[sl])
                    PB = poolB[:, sl * TT:(sl + 1) * TT]
                    ins = vector.tensor_tensor(
                        PB, gg[:, k * TT:(k + 1) * TT], HBk, OP.mult)
                    y32_tok = ("ve", c.inc(ins, "ve"))
                    free_hb[k] = y32_tok
                    c.needt(vector, "ve", free_xc_pe[0])
                    ins = vector.tensor_copy(xch[:, k * TT:(k + 1) * TT], PB)
                    c.inc(ins, "ve")
                    ins = vector.tensor_sub(
                        xcl[:, k * TT:(k + 1) * TT], PB,
                        xch[:, k * TT:(k + 1) * TT])
                    y_toks[k] = ("ve", c.inc(ins, "ve"))
                    free_B[sl] = y_toks[k]
                    if t + 1 < nt:
                        emit_xload(t + 1, k)

                g2_order = []
                for p2 in range(4):
                    g2_order += [2 * p2, 2 * p2 + 1, 8 + 2 * p2, 9 + 2 * p2]
                for j2, m in enumerate(g2_order):
                    par = (t * NH + 8 + j2 // 2) % 3
                    bidx = next_bank()
                    gtok = mm_group(bidx, par, (j2 % 2) * P, xch, xcl,
                                    extra=[xc_rdy])
                    if j2 % 2 == 1:
                        whl_free[par] = gtok
                        load_next_half()
                    if j2 == 15:
                        free_xc_pe[0] = gtok
                    c.needt(scalar, "sc", gtok)
                    if m < 8:
                        c.need(scalar, "sc", "ve", free_sf[m][1]) if free_sf[m] else None
                        ins = scalar.activation(
                            sf[:, m * TT:(m + 1) * TT], pg[bidx][:, :],
                            AF.Sigmoid, bias=vecs[:, C_BGF + m:C_BGF + m + 1])
                    else:
                        c.needt(scalar, "sc", free_si[0])
                        ins = scalar.activation(
                            si[:, (m - 8) * TT:(m - 7) * TT], pg[bidx][:, :],
                            AF.Sigmoid,
                            bias=vecs[:, C_BGI + (m - 8):C_BGI + (m - 8) + 1])
                    bank_free[bidx] = ("sc", c.inc(ins, "sc"))
                    if j2 % 4 == 3:
                        p2 = j2 // 4
                        emit_EL_a(2 * p2)
                        emit_EL_a(2 * p2 + 1)

                silu_toks = [None] * KC

                def emit_silu(ii):
                    c.needt(scalar, "sc", acc_toks[ii])
                    c.need(scalar, "sc", "ve", free_hb[ii][1])
                    ins = scalar.activation(
                        gg[:, ii * TT:(ii + 1) * TT], accb[:, :], AF.Silu)
                    silu_toks[ii] = ("sc", c.inc(ins, "sc"))

                acc_toks = [None] * KC
                for i in range(8):
                    L_toks = [None] * 8
                    C_toks = [None] * 8
                    for n in range(8):
                        par = (t * NH + 16 + 8 * i + n // 2) % 3
                        bidx = next_bank()
                        gtok = mm_group(bidx, par, (n % 2) * P, xnh, xnl)
                        if n % 2 == 1:
                            whl_free[par] = gtok
                            load_next_half()
                        if i == 0:
                            emit_EL_b(n)
                        c.needt(scalar, "sc", gtok)
                        if free_sf[n] is not None:
                            c.needt(scalar, "sc", free_sf[n])
                        ins = scalar.activation(
                            sf[:, n * TT:(n + 1) * TT], pg[bidx][:, :],
                            AF.Copy)
                        L_toks[n] = ("sc", c.inc(ins, "sc"))
                        bank_free[bidx] = L_toks[n]
                    for n in range(8):
                        par = (t * NH + 16 + 8 * i + 4 + n // 2) % 3
                        bidx = next_bank()
                        gtok = mm_group(bidx, par, (n % 2) * P, xnh, xnl)
                        if n % 2 == 1:
                            whl_free[par] = gtok
                            load_next_half()
                        if i == 7 and n == 7:
                            free_xn[0] = gtok
                        c.needt(vector, "ve", gtok)
                        ins = vector.tensor_scalar(
                            si[:, n * TT:(n + 1) * TT], pg[bidx][:, :],
                            vecs[:, C_CMPB + i * 8 + n:C_CMPB + i * 8 + n + 1],
                            None, OP.add)
                        C_toks[n] = ("ve", c.inc(ins, "ve"))
                        bank_free[bidx] = C_toks[n]
                    if i == 6 and t + 1 < nt:
                        ln_rows_box[0] = emit_LN(t + 1, preloaded=True)
                    if i > 0:
                        emit_silu(i - 1)
                    c.needt(vector, "ve", L_toks[7])
                    ins = vector.tensor_scalar(
                        mxb[:, :], sf[:, 0:TT],
                        vecs[:, C_SELB + i * 8:C_SELB + i * 8 + 1],
                        None, OP.add)
                    c.inc(ins, "ve")
                    for n in range(1, 8):
                        ins = vector.scalar_tensor_tensor(
                            mxb[:, :], sf[:, n * TT:(n + 1) * TT],
                            vecs[:, C_SELB + i * 8 + n:C_SELB + i * 8 + n + 1],
                            mxb[:, :], OP.add, OP.max)
                        c.inc(ins, "ve")
                    for n in range(8):
                        ins = vector.scalar_tensor_tensor(
                            msk[:, n * TT:(n + 1) * TT],
                            sf[:, n * TT:(n + 1) * TT],
                            vecs[:, C_SELB + i * 8 + n:C_SELB + i * 8 + n + 1],
                            mxb[:, :], OP.add, OP.is_equal)
                        tok = ("ve", c.inc(ins, "ve"))
                        free_sf[n] = tok
                    acc_guard = silu_toks[i - 1] if i > 0 else prev_silu_tile[0]
                    if acc_guard is not None:
                        c.needt(vector, "ve", acc_guard)
                    ins = vector.tensor_copy(accb[:, :], si[:, 7 * TT:8 * TT])
                    c.inc(ins, "ve")
                    for n in range(6, -1, -1):
                        ins = vector.copy_predicated(
                            accb[:, :], msk[:, n * TT:(n + 1) * TT],
                            si[:, n * TT:(n + 1) * TT])
                        tok = ("ve", c.inc(ins, "ve"))
                    free_si[0] = tok
                    acc_toks[i] = ("ve", tok[1])
                    if i == 7:
                        emit_silu(7)
                        prev_silu_tile[0] = silu_toks[7]

                y_rdy = y_toks[KC - 1]

                s_toks = [None] * KC
                combine_done = [False] * 8

                def emit_combine(i, t0=t0, s_toks=s_toks,
                                 silu_toks=silu_toks):
                    combine_done[i] = True
                    xsl = i % 2
                    c.needt(sync, "sync", free_A[xsl])
                    ins = sync.dma_start(
                        out=poolA[:, xsl * TT:(xsl + 1) * TT],
                        in_=xT[i * P:(i + 1) * P, t0:t0 + TT])
                    xr_tok = ("gr", c.inc(ins, "gr", 16))
                    osl = i % 2
                    gon = "go" + str(osl)
                    if t > 0 or i > 1:
                        c.need(vector, "ve", gon, c.cnt[gon])
                    c.needt(vector, "ve", silu_toks[i])
                    c.needt(vector, "ve", s_toks[i])
                    OBi = poolC[:, osl * TT:(osl + 1) * TT]
                    ins = vector.tensor_tensor(
                        OBi, al[:, i * TT:(i + 1) * TT],
                        gg[:, i * TT:(i + 1) * TT], OP.mult)
                    tok = ("ve", c.inc(ins, "ve"))
                    free_al[i] = tok
                    free_gg[i] = tok
                    c.needt(vector, "ve", xr_tok)
                    c.need(vector, "ve", "gr", c.cnt["gr"])
                    ins = vector.tensor_tensor(
                        OBi, OBi, poolA[:, xsl * TT:(xsl + 1) * TT], OP.add)
                    otok = ("ve", c.inc(ins, "ve"))
                    free_A[xsl] = otok
                    c.need(sync, "sync", "ve", otok[1])
                    ins = sync.dma_start(
                        out=outT[i * P:(i + 1) * P, t0:t0 + TT], in_=OBi)
                    c.inc(ins, gon, 16)
                    free_C[osl] = (gon, c.cnt[gon])

                for m in range(8):
                    par = (t * NH + 80 + m // 2) % 3
                    bidx = next_bank()
                    gtok = mm_group(bidx, par, (m % 2) * P, xch, xcl,
                                    extra=[y_rdy])
                    if m % 2 == 1:
                        whl_free[par] = gtok
                        load_next_half()
                    if m == 1 and t + 1 < nt:
                        xn_rdy_box[0] = emit_LN_post(t + 1)
                    if m == 7:
                        free_y_pe[0] = gtok
                    c.needt(scalar, "sc", gtok)
                    c.need(scalar, "sc", "ve", scan_toks[m][1])
                    ins = scalar.activation(
                        al[:, m * TT:(m + 1) * TT], pg[bidx][:, :], AF.Copy)
                    s_toks[m] = ("sc", c.inc(ins, "sc"))
                    bank_free[bidx] = s_toks[m]
                    if t == nt - 1:
                        emit_combine(m)

                if t < nt - 1:
                    pending_combines[0] = emit_combine

            gpsimd.wait_ge(c.h["go0"], c.cnt["go0"])
            gpsimd.wait_ge(c.h["go1"], c.cnt["go1"])
            gpsimd.wait_ge(c.h["gx"], c.cnt["gx"])
            gpsimd.wait_ge(c.h["gr"], c.cnt["gr"])
            gpsimd.wait_ge(c.h["gv"], c.cnt["gv"])
            for sg in ("gw0", "gw1", "gw2"):
                gpsimd.wait_ge(c.h[sg], c.cnt[sg])
            for s4 in ("wd0a", "wd0b", "wd1a", "wd1b", "wds0", "wds1", "wds2"):
                sync.wait_ge(c.h[s4], c.cnt[s4])

    return nc


def _prep(L0_attn_x, L0_attn_w_in, L0_attn_w_conv, L0_attn_b_conv,
          L0_attn_w_gates, L0_attn_b_gates, L0_attn_forget_base,
          L0_attn_w_out, L0_ffn_sel_w, L0_ffn_sel_b, L0_ffn_comp_w,
          L0_ffn_comp_b):
    A = lambda v: np.ascontiguousarray(np.asarray(v, dtype=np.float32))
    x = A(L0_attn_x)
    xT = np.ascontiguousarray(x.transpose(0, 2, 1))
    w1T = A(np.asarray(L0_attn_w_in).T)
    w2T = A(np.asarray(L0_attn_w_gates).T)
    w3T = A(np.asarray(L0_attn_w_out).T)
    sel = A(L0_ffn_sel_w)
    selT = sel.T.reshape(D, 1024, 8)
    selT = selT.reshape(D, 8, 128, 8)
    selT = np.ascontiguousarray(selT.transpose(0, 1, 3, 2).reshape(D, 8192))
    cw = A(L0_ffn_comp_w)
    cmpT = cw.transpose(2, 1, 0)
    cmpT = cmpT.reshape(D, 8, 128, 8)
    cmpT = np.ascontiguousarray(cmpT.transpose(0, 1, 3, 2).reshape(D, 8192))
    vecs = np.zeros((P, VW), dtype=np.float32)
    vecs[:, C_FB:C_FB + KC] = A(L0_attn_forget_base).reshape(KC, P).T
    vecs[:, C_BCONV:C_BCONV + KC] = A(L0_attn_b_conv).reshape(KC, P).T
    wc = A(L0_attn_w_conv).reshape(D, 4).reshape(KC, P, 4)
    for k in range(KC):
        vecs[:, C_WCONV + k * 4:C_WCONV + (k + 1) * 4] = wc[k]
    bg = A(L0_attn_b_gates)
    vecs[:, C_BGF:C_BGF + KC] = bg[:D].reshape(KC, P).T
    vecs[:, C_BGI:C_BGI + KC] = bg[D:].reshape(KC, P).T
    sbr = A(L0_ffn_sel_b).reshape(1024, 8).reshape(KC, P, 8)
    cbr = A(L0_ffn_comp_b).T.reshape(KC, P, 8)
    for i in range(KC):
        for n in range(8):
            vecs[:, C_SELB + i * 8 + n] = sbr[i, :, n]
            vecs[:, C_CMPB + i * 8 + n] = cbr[i, :, n]
    for col, val in ((C_M8, -8.0), (C_M16, -16.0), (C_INVD, 1.0 / D),
                     (C_EPS, EPS), (C_MH, -0.5), (C_M1, -1.0), (C_H, 0.5)):
        vecs[:, col] = val
    return xT, w1T, w2T, w3T, selT, cmpT, vecs


def kernel(**inputs):
    xT, w1T, w2T, w3T, selT, cmpT, vecs = _prep(**inputs)
    nc = build(nt=4)
    in_maps = []
    for cid in range(8):
        in_maps.append({
            "xT": xT[cid], "w1T": w1T, "w2T": w2T, "w3T": w3T,
            "selT": selT, "cmpT": cmpT, "vecs": vecs,
        })
    res = run_bass_kernel_spmd(nc, in_maps, core_ids=list(range(8)))
    out = np.stack([np.asarray(res.results[cid]["outT"]).T
                    for cid in range(8)])
    return np.ascontiguousarray(out.astype(np.float32))


# version 24
# speedup vs baseline: 1.1443x; 1.0041x over previous
import numpy as np
import concourse.bass as bass
import concourse.mybir as mybir
from concourse.bass_utils import run_bass_kernel_spmd

F32 = mybir.dt.float32
BF16 = mybir.dt.bfloat16
U8 = mybir.dt.uint8
AF = mybir.ActivationFunctionType
OP = mybir.AluOpType

P = 128
D = 1024
KC = 8
T = 2048
TT = 512
HW = 256
EPS = 1e-5

C_FB = 0
C_BCONV = 8
C_WCONV = 16
C_BGF = 48
C_BGI = 56
C_SELB = 64
C_CMPB = 128
C_M8 = 192
C_M16 = 193
C_INVD = 194
C_EPS = 195
C_MH = 196
C_M1 = 197
C_H = 198
VW = 200


class Ctl:
    def __init__(self, nc, es):
        self.nc = nc
        self.h = {}
        self.cnt = {}
        self.waited = {}
        for name in ("wd0a", "wd0b", "wd1a", "wd1b",
                     "gw0", "gw1", "gw2", "wds0", "wds1", "wds2", "wds3", "wds4",
                     "gx", "gr", "go0", "go1", "gv",
                     "pe", "sc", "ve", "gpc"):
            self.h[name] = es.enter_context(nc.semaphore(name))
            self.cnt[name] = 0

    def need(self, eng, engname, sem, val):
        if val is None or val <= 0:
            return
        key = (engname, sem)
        if self.waited.get(key, 0) >= val:
            return
        assert val <= self.cnt[sem], (
            f"wait {sem}>={val} but only {self.cnt[sem]} incs emitted")
        eng.wait_ge(self.h[sem], val)
        self.waited[key] = val

    def needt(self, eng, engname, tok):
        if tok is not None:
            self.need(eng, engname, tok[0], tok[1])

    def inc(self, instr, sem, n=1):
        instr.then_inc(self.h[sem], n)
        self.cnt[sem] += n
        return self.cnt[sem]


class Chained:

    def __init__(self, eng, engname, sem, ctl):
        self._eng = eng
        self._engname = engname
        self._sem = sem
        self._ctl = ctl

    def __getattr__(self, name):
        fn = getattr(self._eng, name)
        if name in ("wait_ge", "nop"):
            return fn

        def wrapped(*a, **k):
            ctl = self._ctl
            ctl.need(self._eng, self._engname, self._sem, ctl.cnt[self._sem])
            return fn(*a, **k)
        return wrapped


def build(nt=4):
    nc = bass.Bass("TRN2", target_bir_lowering=False, debug=False)

    xT = nc.dram_tensor("xT", [D, T], F32, kind="ExternalInput").ap()
    w1T = nc.dram_tensor("w1T", [D, 2048], F32, kind="ExternalInput").ap()
    w2T = nc.dram_tensor("w2T", [D, 2048], F32, kind="ExternalInput").ap()
    w3T = nc.dram_tensor("w3T", [D, 1024], F32, kind="ExternalInput").ap()
    selT = nc.dram_tensor("selT", [D, 8192], F32, kind="ExternalInput").ap()
    cmpT = nc.dram_tensor("cmpT", [D, 8192], F32, kind="ExternalInput").ap()
    vecs_d = nc.dram_tensor("vecs", [P, VW], F32, kind="ExternalInput").ap()
    outT = nc.dram_tensor("outT", [D, T], F32, kind="ExternalOutput").ap()
    wscr = nc.dram_tensor("wscr", [84 * P, 2 * 8 * 256], BF16).ap()

    half_list = []
    for s in (2, 3, 0, 1):
        half_list.append((w1T, s * TT))
        half_list.append((w1T, s * TT + HW))
    for p2 in range(4):
        half_list.append((w2T, p2 * HW))
        half_list.append((w2T, 1024 + p2 * HW))
    for i in range(8):
        for s in (2 * i, 2 * i + 1):
            half_list.append((selT, s * TT))
            half_list.append((selT, s * TT + HW))
        for s in (2 * i, 2 * i + 1):
            half_list.append((cmpT, s * TT))
            half_list.append((cmpT, s * TT + HW))
    for s in range(2):
        half_list.append((w3T, s * TT))
        half_list.append((w3T, s * TT + HW))
    NH = len(half_list)

    from contextlib import ExitStack
    es = ExitStack()
    with es:
        sb = lambda n, s, d=F32: es.enter_context(nc.sbuf_tensor(n, s, d))
        w32 = [sb("w320", [P, 8 * HW]), sb("w321", [P, 8 * HW])]
        wh = [sb("wh0", [P, 8 * HW], BF16), sb("wh1", [P, 8 * HW], BF16),
              sb("wh2", [P, 8 * HW], BF16)]
        wl = [sb("wl0", [P, 8 * HW], BF16), sb("wl1", [P, 8 * HW], BF16),
              sb("wl2", [P, 8 * HW], BF16)]
        xnh = sb("xnh", [P, 8 * TT], BF16)
        xnl = sb("xnl", [P, 8 * TT], BF16)
        xch = sb("xch", [P, 8 * TT], BF16)
        xcl = sb("xcl", [P, 8 * TT], BF16)
        gg = sb("gg", [P, 8 * TT])
        sf = sb("sf", [P, 8 * TT])
        si = sb("si", [P, 8 * TT])
        al = sb("al", [P, 8 * TT])
        hb = sb("hb", [P, 8 * TT])
        XAW = 520
        xa = sb("xa", [P, 8 * XAW])
        vecs = sb("vecsb", [P, VW])
        ce = sb("ce", [P, KC])
        ce2 = sb("ce2", [P, KC])
        hcarry = sb("hcarry", [P, KC])
        rows = sb("rows", [1, 4 * TT])
        poolA = sb("poolA", [P, 2 * TT])
        poolB = sb("poolB", [P, 2 * TT])
        poolC = sb("poolC", [P, 2 * TT])
        scr = sb("scr", [P, 2 * TT])
        scr2 = sb("scr2", [P, 2 * TT])
        mxb = sb("mxb", [P, TT])
        accb = sb("accb", [P, TT])
        msk = sb("msk", [P, 8 * TT], U8)
        ones_c = sb("ones_c", [1, P])
        ones_r = sb("ones_r", [P, 1])

        pg = [es.enter_context(nc.psum_tensor(f"pg{i}", [P, TT], F32))
              for i in range(8)]

        c = Ctl(nc, es)
        block = es.enter_context(nc.Block())

        free_xn = [None]
        free_gg = [None] * KC
        free_hb = [None] * KC
        free_sf = [None] * KC
        free_si = [None]
        free_al = [None] * KC
        free_xa = [None] * KC
        free_xc_pe = [None]
        free_y_pe = [None]
        u_toks = [None] * KC
        free_A = [None, None]
        free_B = [None, None]
        free_C = [None, None]
        free_scr = [None, None]
        free_scr2 = [None, None]
        free_rows = [None]
        bank_free = [None] * 8
        whl_free = [None] * 5
        w32_free = [[], []]
        w_ready = [[], [], [], [], []]
        wd_loaded = [0, 0]
        half_seq = [0]
        prev_silu = [None]
        prev_silu_tile = [None]
        scr_dma_tok = {}
        pending_combines = [None]

        @block.vector
        def _(vector):
            tensor = nc.tensor
            scalar = Chained(nc.scalar, "sc", "sc", c)
            vector = Chained(nc.vector, "ve", "ve", c)
            gpsimd = Chained(nc.gpsimd, "gp", "gpc", c)
            sync = nc.sync

            dma_seq = [0]
            split_seq = [0]
            w32b = [w32[0][:, :].bitcast(BF16), w32[1][:, :].bitcast(BF16)]

            def cpar(g):
                return g % 3 if g < NH else (g - NH) % 5

            def whl_bufs(par):
                if par < 3:
                    return wh[par][:, :], wl[par][:, :]
                v = w32b[par - 3]
                return v[:, 0:8 * HW], v[:, 8 * HW:16 * HW]

            def pump_dma():
                hs = dma_seq[0]
                if hs >= nt * NH:
                    return
                dma_seq[0] += 1
                q = hs % 2
                HF = 4 * HW
                if hs >= NH:
                    return
                dram, col = half_list[hs % NH]
                rr = dram.rearrange("(k p) c -> p k c", p=P)
                for tok in w32_free[q]:
                    c.needt(sync, "sync", tok)
                for pc in range(2):
                    wdn = "wd" + str(q) + ("a" if pc == 0 else "b")
                    src3 = rr[:, 4 * pc:4 * pc + 4, col:col + HW]
                    ins = sync.dma_start(
                        out=w32[q][:, pc * HF:(pc + 1) * HF].rearrange(
                            "p (k c) -> p k c", c=HW),
                        in_=src3)
                    c.inc(ins, wdn, 16)

            def pump_split():
                hs = split_seq[0]
                if hs >= nt * NH:
                    return
                assert hs < dma_seq[0]
                split_seq[0] += 1
                par = cpar(hs)
                q = hs % 2
                HF = 4 * HW
                if hs >= NH:
                    hrow = hs % NH
                    wdn = "wds" + str(par)
                    if par >= 3 and whl_free[par] is None:
                        for tok in w32_free[par - 3]:
                            c.needt(sync, "sync", tok)
                    c.needt(sync, "sync", whl_free[par])
                    c.need(sync, "sync", "gw" + str(hrow % 3),
                           c.cnt["gw" + str(hrow % 3)])
                    bh, bl = whl_bufs(par)
                    for pc, buf in ((0, bh), (1, bl)):
                        ins = sync.dma_start(
                            out=buf,
                            in_=wscr[hrow * P:(hrow + 1) * P,
                                     pc * 2048:(pc + 1) * 2048])
                        c.inc(ins, wdn, 16)
                    w_ready[par] = [(wdn, c.cnt[wdn])]
                    return
                lo_toks = []
                for pc in range(2):
                    wdn = "wd" + str(q) + ("a" if pc == 0 else "b")
                    c.need(scalar, "sc", wdn, c.cnt[wdn])
                    c.needt(scalar, "sc", whl_free[par])
                    c.need(scalar, "sc", "gw" + str(par), c.cnt["gw" + str(par)])
                    ins = scalar.activation(
                        wh[par][:, pc * HF:(pc + 1) * HF],
                        w32[q][:, pc * HF:(pc + 1) * HF], AF.Copy)
                    hi = ("sc", c.inc(ins, "sc"))
                    c.needt(gpsimd, "gp", hi)
                    c.needt(gpsimd, "gp", whl_free[par])
                    c.need(gpsimd, "gp", wdn, c.cnt[wdn])
                    ins = gpsimd.tensor_tensor(
                        wl[par][:, pc * HF:(pc + 1) * HF],
                        w32[q][:, pc * HF:(pc + 1) * HF],
                        wh[par][:, pc * HF:(pc + 1) * HF], OP.subtract)
                    lo_toks.append(("gpc", c.inc(ins, "gpc")))
                w32_free[q] = lo_toks
                w_ready[par] = lo_toks
                gwn = "gw" + str(par)
                for pc, buf in ((0, wh), (1, wl)):
                    ins = gpsimd.dma_start(
                        out=wscr[hs * P:(hs + 1) * P, pc * 2048:(pc + 1) * 2048],
                        in_=buf[par][:, :])
                    c.inc(ins, gwn, 16)

            def load_next_half():
                pump_split()
                pump_dma()

            rot = [0]

            def next_bank():
                b = rot[0] % 8
                rot[0] += 1
                return b

            def mm_group(bidx, par, col, rh, rl, extra=None):
                bank = pg[bidx]
                c.needt(tensor, "pe", bank_free[bidx])
                for tok in w_ready[par]:
                    c.needt(tensor, "pe", tok)
                if extra:
                    for tok in extra:
                        c.needt(tensor, "pe", tok)
                last = None
                idx = 0
                bufh, bufl = whl_bufs(par)
                for k in range(KC):
                    wcol = k * HW + col
                    for wop, xop in ((bufh, rh), (bufh, rl), (bufl, rh)):
                        last = tensor.matmul(
                            bank[:, :], wop[:, wcol:wcol + P],
                            xop[:, k * TT:(k + 1) * TT],
                            start=(idx == 0), stop=(idx == 23))
                        idx += 1
                return ("pe", c.inc(last, "pe"))

            for apx, val in [(ones_c[:, :], 1.0), (ones_r[:, :], 1.0),
                             (hcarry[:, :], 0.0)]:
                c.inc(gpsimd.memset(apx, val), "gpc")
            for k in range(KC):
                c.inc(gpsimd.memset(xa[:, k * XAW:k * XAW + 3], 0.0), "gpc")
            tok_init = ("gpc", c.cnt["gpc"])
            ins = gpsimd.dma_start(out=vecs[:, :], in_=vecs_d[:, :])
            vecs_tok = ("gv", c.inc(ins, "gv", 16))
            c.needt(scalar, "sc", vecs_tok)
            c.inc(scalar.activation(ce[:, :], vecs[:, C_FB:C_FB + KC], AF.Exp), "sc")
            c.inc(scalar.activation(ce[:, :], ce[:, :], AF.Ln, bias=1.0), "sc")
            c.inc(scalar.activation(ce2[:, :], ce[:, :], AF.Copy,
                                    scale=vecs[:, C_M16:C_M16 + 1]), "sc")
            ins = scalar.activation(ce[:, :], ce[:, :], AF.Copy,
                                    scale=vecs[:, C_M8:C_M8 + 1])
            ce_tok = ("sc", c.inc(ins, "sc"))
            c.needt(vector, "ve", vecs_tok)
            c.needt(vector, "ve", tok_init)
            c.needt(tensor, "pe", tok_init)

            pump_dma()
            pump_dma()
            pump_split()
            pump_split()
            pump_dma()
            pump_split()
            pump_dma()

            def emit_xload(t, k):
                t0 = t * TT
                c.needt(gpsimd, "gp", free_hb[k])
                ins = gpsimd.dma_start(
                    out=hb[:, k * TT:(k + 1) * TT],
                    in_=xT[k * P:(k + 1) * P, t0:t0 + TT])
                c.inc(ins, "gx", 16)

            def emit_LN(t, preloaded=False):
                t0 = t * TT
                if not preloaded:
                    for k in range(KC):
                        emit_xload(t, k)
                xtok = ("gx", c.cnt["gx"])
                bsum, bsq = next_bank(), next_bank()
                sum_tok = sq_tok = None
                for k in range(KC):
                    sl = k % 2
                    c.needt(gpsimd, "gp", xtok)
                    c.needt(gpsimd, "gp", free_A[sl])
                    ins = gpsimd.tensor_tensor(
                        poolA[:, sl * TT:(sl + 1) * TT],
                        hb[:, k * TT:(k + 1) * TT],
                        hb[:, k * TT:(k + 1) * TT], OP.mult)
                    sq_rdy = ("gpc", c.inc(ins, "gpc"))
                    c.needt(tensor, "pe", xtok)
                    if k == 0:
                        c.needt(tensor, "pe", bank_free[bsum])
                        c.needt(tensor, "pe", bank_free[bsq])
                    ins = tensor.matmul(pg[bsum][0:1, :], ones_r[:, :],
                                        hb[:, k * TT:(k + 1) * TT],
                                        start=(k == 0), stop=(k == KC - 1))
                    if k == KC - 1:
                        sum_tok = ("pe", c.inc(ins, "pe"))
                    c.needt(tensor, "pe", sq_rdy)
                    ins = tensor.matmul(pg[bsq][0:1, :], ones_r[:, :],
                                        poolA[:, sl * TT:(sl + 1) * TT],
                                        start=(k == 0), stop=(k == KC - 1))
                    free_A[sl] = sq_tok = ("pe", c.inc(ins, "pe"))
                R = lambda j: rows[:, j * TT:(j + 1) * TT]
                c.needt(scalar, "sc", sum_tok)
                c.needt(scalar, "sc", sq_tok)
                c.needt(scalar, "sc", free_rows[0])
                ins = scalar.activation(R(0), pg[bsum][0:1, :], AF.Copy,
                                        scale=vecs[0:1, C_INVD:C_INVD + 1])
                bank_free[bsum] = ("sc", c.inc(ins, "sc"))
                ins = scalar.activation(R(1), pg[bsq][0:1, :], AF.Copy,
                                        scale=vecs[0:1, C_INVD:C_INVD + 1])
                bank_free[bsq] = ("sc", c.inc(ins, "sc"))
                ins = scalar.activation(R(2), R(0), AF.Square)
                m2_tok = ("sc", c.inc(ins, "sc"))
                c.needt(vector, "ve", m2_tok)
                ins = vector.tensor_tensor(R(1), R(1), R(2), OP.subtract)
                var_tok = ("ve", c.inc(ins, "ve"))
                c.needt(scalar, "sc", var_tok)
                ins = scalar.activation(R(3), R(1), AF.Ln,
                                        bias=vecs[0:1, C_EPS:C_EPS + 1])
                c.inc(ins, "sc")
                ins = scalar.activation(R(2), R(3), AF.Exp,
                                        scale=vecs[0:1, C_MH:C_MH + 1])
                rstd_tok = ("sc", c.inc(ins, "sc"))
                c.needt(vector, "ve", rstd_tok)
                ins = vector.tensor_tensor(R(0), R(0), R(2), OP.mult)
                c.inc(ins, "ve")
                c.need(scalar, "sc", "ve", c.cnt["ve"])
                ins = scalar.activation(R(3), R(0), AF.Copy,
                                        scale=vecs[0:1, C_M1:C_M1 + 1])
                brow_tok = ("sc", c.inc(ins, "sc"))
                return (rstd_tok, brow_tok)

            def emit_LN_post(t):
                rstd_tok, brow_tok = ln_rows_box[0]
                R = lambda j: rows[:, j * TT:(j + 1) * TT]
                bA, bB = next_bank(), next_bank()
                c.needt(tensor, "pe", rstd_tok)
                c.needt(tensor, "pe", brow_tok)
                c.needt(tensor, "pe", bank_free[bA])
                c.needt(tensor, "pe", bank_free[bB])
                ins = tensor.matmul(pg[bA][:, :], ones_c[:, :], R(2),
                                    start=True, stop=True)
                c.inc(ins, "pe")
                ins = tensor.matmul(pg[bB][:, :], ones_c[:, :], R(3),
                                    start=True, stop=True)
                bcB_tok = ("pe", c.inc(ins, "pe"))
                free_rows[0] = bcB_tok
                xn_last = None
                for k in range(KC):
                    sl = k % 2
                    c.needt(vector, "ve", bcB_tok)
                    c.needt(vector, "ve", free_xn[0])
                    c.needt(vector, "ve", free_B[sl])
                    PB = poolB[:, sl * TT:(sl + 1) * TT]
                    ins = vector.tensor_tensor(
                        PB, hb[:, k * TT:(k + 1) * TT], pg[bA][:, :], OP.mult)
                    c.inc(ins, "ve")
                    ins = vector.tensor_tensor(PB, PB, pg[bB][:, :], OP.add)
                    c.inc(ins, "ve")
                    ins = vector.tensor_copy(xnh[:, k * TT:(k + 1) * TT], PB)
                    c.inc(ins, "ve")
                    ins = vector.tensor_sub(
                        xnl[:, k * TT:(k + 1) * TT], PB,
                        xnh[:, k * TT:(k + 1) * TT])
                    xn_last = ("ve", c.inc(ins, "ve"))
                    free_B[sl] = xn_last
                bank_free[bA] = bank_free[bB] = xn_last
                return xn_last

            ln_rows_box = [None]
            ln_rows_box[0] = emit_LN(0)
            xn_rdy_box = [None]
            xn_rdy_box[0] = emit_LN_post(0)

            for t in range(nt):
                t0 = t * TT
                xn_rdy = xn_rdy_box[0]

                for j, m in enumerate(list(range(8, 16)) + list(range(8))):
                    par = cpar(t * NH + j // 2)
                    bidx = next_bank()
                    gtok = mm_group(bidx, par, (j % 2) * P, xnh, xnl,
                                    extra=[xn_rdy])
                    if j % 2 == 1:
                        whl_free[par] = gtok
                        load_next_half()
                    if m < 8:
                        c.needt(scalar, "sc", gtok)
                        c.needt(scalar, "sc", free_gg[m])
                        ins = scalar.activation(
                            gg[:, m * TT:(m + 1) * TT], pg[bidx][:, :], AF.Gelu)
                        bank_free[bidx] = ("sc", c.inc(ins, "sc"))
                    else:
                        k = m - 8
                        tail_tok = tok_init
                        if t > 0:
                            c.needt(vector, "ve", free_xa[k])
                            ins = vector.tensor_copy(
                                xa[:, k * XAW:k * XAW + 3],
                                xa[:, k * XAW + TT:k * XAW + TT + 3])
                            tail_tok = ("ve", c.inc(ins, "ve"))
                        c.needt(scalar, "sc", gtok)
                        c.needt(scalar, "sc", tail_tok)
                        ins = scalar.activation(
                            xa[:, k * XAW + 3:k * XAW + 3 + TT],
                            pg[bidx][:, :], AF.Copy)
                        ev_tok = ("sc", c.inc(ins, "sc"))
                        bank_free[bidx] = ev_tok
                        sl = k % 2
                        c.needt(vector, "ve", ev_tok)
                        if free_C[sl] is not None and free_C[sl][0].startswith("go"):
                            c.need(vector, "ve", free_C[sl][0],
                                   c.cnt[free_C[sl][0]])
                        else:
                            c.needt(vector, "ve", free_C[sl])
                        PC = poolC[:, sl * TT:(sl + 1) * TT]
                        ins = vector.tensor_scalar(
                            PC, xa[:, k * XAW + 3:k * XAW + 3 + TT],
                            vecs[:, C_WCONV + k * 4 + 3:C_WCONV + k * 4 + 4],
                            vecs[:, C_BCONV + k:C_BCONV + k + 1],
                            OP.mult, OP.add)
                        c.inc(ins, "ve")
                        for j in (2, 1, 0):
                            ins = vector.scalar_tensor_tensor(
                                PC, xa[:, k * XAW + j:k * XAW + j + TT],
                                vecs[:, C_WCONV + k * 4 + j:C_WCONV + k * 4 + j + 1],
                                PC, OP.mult, OP.add)
                            tok = ("ve", c.inc(ins, "ve"))
                        free_xa[k] = tok
                        c.needt(vector, "ve", free_y_pe[0])
                        ins = vector.tensor_copy(
                            xch[:, k * TT:(k + 1) * TT], PC)
                        c.inc(ins, "ve")
                        ins = vector.tensor_sub(
                            xcl[:, k * TT:(k + 1) * TT], PC,
                            xch[:, k * TT:(k + 1) * TT])
                        xc_rdy = ("ve", c.inc(ins, "ve"))
                        free_C[sl] = xc_rdy
                        if pending_combines[0] is not None:
                            pending_combines[0](k)
                            if k == 7:
                                pending_combines[0] = None
                c.needt(scalar, "sc", ce_tok)
                scan_toks = [None] * KC
                y_toks = [None] * KC

                def emit_EL_a(k):
                    sl = k % 2
                    SFk = sf[:, k * TT:(k + 1) * TT]
                    SIk = si[:, k * TT:(k + 1) * TT]
                    ALk = al[:, k * TT:(k + 1) * TT]
                    SC = scr[:, sl * TT:(sl + 1) * TT]
                    SC2 = scr2[:, sl * TT:(sl + 1) * TT]
                    SC3 = poolC[:, sl * TT:(sl + 1) * TT]
                    c.needt(scalar, "sc", free_al[k])
                    ins = scalar.activation(ALk, SFk, AF.Exp,
                                            scale=ce[:, k:k + 1])
                    al_tok = ("sc", c.inc(ins, "sc"))
                    c.needt(scalar, "sc", free_scr[sl])
                    ins = scalar.activation(SC, SFk, AF.Exp,
                                            scale=ce2[:, k:k + 1])
                    c.inc(ins, "sc")
                    c.need(vector, "ve", "sc", c.cnt["sc"])
                    ins = vector.tensor_scalar(SC, SC, -1.0, 1.0 + 1e-6,
                                               OP.mult, OP.add)
                    ztok = ("ve", c.inc(ins, "ve"))
                    c.needt(scalar, "sc", ztok)
                    ins = scalar.activation(SC, SC, AF.Ln)
                    c.inc(ins, "sc")
                    ins = scalar.activation(SFk, SC, AF.Exp,
                                            scale=vecs[:, C_H:C_H + 1])
                    beta_tok = ("sc", c.inc(ins, "sc"))
                    free_scr[sl] = beta_tok
                    c.needt(vector, "ve", beta_tok)
                    c.needt(vector, "ve", free_scr2[sl])
                    ins = vector.tensor_tensor(SC2, SFk, SIk, OP.mult)
                    ut_tok = ("ve", c.inc(ins, "ve"))
                    free_sf[k] = ut_tok
                    c.need(vector, "ve", "go0", c.cnt["go0"])
                    c.need(vector, "ve", "go1", c.cnt["go1"])
                    ins = vector.tensor_tensor(
                        SC3, xch[:, k * TT:(k + 1) * TT],
                        xcl[:, k * TT:(k + 1) * TT], OP.add)
                    c.inc(ins, "ve")
                    ins = vector.tensor_tensor(SIk, SC2, SC3, OP.mult)
                    u_tok = ("ve", c.inc(ins, "ve"))
                    free_scr2[sl] = u_tok
                    u_toks[k] = u_tok
                    c.needt(vector, "ve", al_tok)
                    HBk = hb[:, k * TT:(k + 1) * TT]
                    ins = vector.tensor_tensor_scan(
                        HBk, ALk, SIk, hcarry[:, k:k + 1], OP.mult, OP.add)
                    scan_toks[k] = ("ve", c.inc(ins, "ve"))
                    free_al[k] = scan_toks[k]
                    ins = vector.tensor_copy(
                        hcarry[:, k:k + 1], hb[:, k * TT + TT - 1:k * TT + TT])
                    c.inc(ins, "ve")

                def emit_EL_b(k):
                    sl = k % 2
                    HBk = hb[:, k * TT:(k + 1) * TT]
                    c.needt(vector, "ve", free_B[sl])
                    PB = poolB[:, sl * TT:(sl + 1) * TT]
                    ins = vector.tensor_tensor(
                        PB, gg[:, k * TT:(k + 1) * TT], HBk, OP.mult)
                    y32_tok = ("ve", c.inc(ins, "ve"))
                    free_hb[k] = y32_tok
                    c.needt(vector, "ve", free_xc_pe[0])
                    ins = vector.tensor_copy(xch[:, k * TT:(k + 1) * TT], PB)
                    c.inc(ins, "ve")
                    ins = vector.tensor_sub(
                        xcl[:, k * TT:(k + 1) * TT], PB,
                        xch[:, k * TT:(k + 1) * TT])
                    y_toks[k] = ("ve", c.inc(ins, "ve"))
                    free_B[sl] = y_toks[k]
                    if t + 1 < nt:
                        emit_xload(t + 1, k)

                g2_order = []
                for p2 in range(4):
                    g2_order += [2 * p2, 2 * p2 + 1, 8 + 2 * p2, 9 + 2 * p2]
                for j2, m in enumerate(g2_order):
                    par = cpar(t * NH + 8 + j2 // 2)
                    bidx = next_bank()
                    gtok = mm_group(bidx, par, (j2 % 2) * P, xch, xcl,
                                    extra=[xc_rdy])
                    if j2 % 2 == 1:
                        whl_free[par] = gtok
                        load_next_half()
                    if j2 == 15:
                        free_xc_pe[0] = gtok
                    c.needt(scalar, "sc", gtok)
                    if m < 8:
                        c.need(scalar, "sc", "ve", free_sf[m][1]) if free_sf[m] else None
                        ins = scalar.activation(
                            sf[:, m * TT:(m + 1) * TT], pg[bidx][:, :],
                            AF.Sigmoid, bias=vecs[:, C_BGF + m:C_BGF + m + 1])
                    else:
                        c.needt(scalar, "sc", free_si[0])
                        ins = scalar.activation(
                            si[:, (m - 8) * TT:(m - 7) * TT], pg[bidx][:, :],
                            AF.Sigmoid,
                            bias=vecs[:, C_BGI + (m - 8):C_BGI + (m - 8) + 1])
                    bank_free[bidx] = ("sc", c.inc(ins, "sc"))
                    if j2 % 4 == 3:
                        p2 = j2 // 4
                        emit_EL_a(2 * p2)
                        emit_EL_a(2 * p2 + 1)

                silu_toks = [None] * KC

                def emit_silu(ii):
                    c.needt(scalar, "sc", acc_toks[ii])
                    c.need(scalar, "sc", "ve", free_hb[ii][1])
                    ins = scalar.activation(
                        gg[:, ii * TT:(ii + 1) * TT], accb[:, :], AF.Silu)
                    silu_toks[ii] = ("sc", c.inc(ins, "sc"))

                acc_toks = [None] * KC
                for i in range(8):
                    L_toks = [None] * 8
                    C_toks = [None] * 8
                    for n in range(8):
                        par = cpar(t * NH + 16 + 8 * i + n // 2)
                        bidx = next_bank()
                        gtok = mm_group(bidx, par, (n % 2) * P, xnh, xnl)
                        if n % 2 == 1:
                            whl_free[par] = gtok
                            load_next_half()
                        if i == 0:
                            emit_EL_b(n)
                        c.needt(scalar, "sc", gtok)
                        if free_sf[n] is not None:
                            c.needt(scalar, "sc", free_sf[n])
                        ins = scalar.activation(
                            sf[:, n * TT:(n + 1) * TT], pg[bidx][:, :],
                            AF.Copy)
                        L_toks[n] = ("sc", c.inc(ins, "sc"))
                        bank_free[bidx] = L_toks[n]
                    for n in range(8):
                        par = cpar(t * NH + 16 + 8 * i + 4 + n // 2)
                        bidx = next_bank()
                        gtok = mm_group(bidx, par, (n % 2) * P, xnh, xnl)
                        if n % 2 == 1:
                            whl_free[par] = gtok
                            load_next_half()
                        if i == 7 and n == 7:
                            free_xn[0] = gtok
                        c.needt(vector, "ve", gtok)
                        ins = vector.tensor_scalar(
                            si[:, n * TT:(n + 1) * TT], pg[bidx][:, :],
                            vecs[:, C_CMPB + i * 8 + n:C_CMPB + i * 8 + n + 1],
                            None, OP.add)
                        C_toks[n] = ("ve", c.inc(ins, "ve"))
                        bank_free[bidx] = C_toks[n]
                    if i == 6 and t + 1 < nt:
                        ln_rows_box[0] = emit_LN(t + 1, preloaded=True)
                    if i > 0:
                        emit_silu(i - 1)
                    c.needt(vector, "ve", L_toks[7])
                    ins = vector.tensor_scalar(
                        mxb[:, :], sf[:, 0:TT],
                        vecs[:, C_SELB + i * 8:C_SELB + i * 8 + 1],
                        None, OP.add)
                    c.inc(ins, "ve")
                    for n in range(1, 8):
                        ins = vector.scalar_tensor_tensor(
                            mxb[:, :], sf[:, n * TT:(n + 1) * TT],
                            vecs[:, C_SELB + i * 8 + n:C_SELB + i * 8 + n + 1],
                            mxb[:, :], OP.add, OP.max)
                        c.inc(ins, "ve")
                    for n in range(8):
                        ins = vector.scalar_tensor_tensor(
                            msk[:, n * TT:(n + 1) * TT],
                            sf[:, n * TT:(n + 1) * TT],
                            vecs[:, C_SELB + i * 8 + n:C_SELB + i * 8 + n + 1],
                            mxb[:, :], OP.add, OP.is_equal)
                        tok = ("ve", c.inc(ins, "ve"))
                        free_sf[n] = tok
                    acc_guard = silu_toks[i - 1] if i > 0 else prev_silu_tile[0]
                    if acc_guard is not None:
                        c.needt(vector, "ve", acc_guard)
                    ins = vector.tensor_copy(accb[:, :], si[:, 7 * TT:8 * TT])
                    c.inc(ins, "ve")
                    for n in range(6, -1, -1):
                        ins = vector.copy_predicated(
                            accb[:, :], msk[:, n * TT:(n + 1) * TT],
                            si[:, n * TT:(n + 1) * TT])
                        tok = ("ve", c.inc(ins, "ve"))
                    free_si[0] = tok
                    acc_toks[i] = ("ve", tok[1])
                    if i == 7:
                        emit_silu(7)
                        prev_silu_tile[0] = silu_toks[7]

                y_rdy = y_toks[KC - 1]

                s_toks = [None] * KC
                combine_done = [False] * 8

                def emit_combine(i, t0=t0, s_toks=s_toks,
                                 silu_toks=silu_toks):
                    combine_done[i] = True
                    xsl = i % 2
                    c.needt(sync, "sync", free_A[xsl])
                    ins = sync.dma_start(
                        out=poolA[:, xsl * TT:(xsl + 1) * TT],
                        in_=xT[i * P:(i + 1) * P, t0:t0 + TT])
                    xr_tok = ("gr", c.inc(ins, "gr", 16))
                    osl = i % 2
                    gon = "go" + str(osl)
                    if t > 0 or i > 1:
                        c.need(vector, "ve", gon, c.cnt[gon])
                    c.needt(vector, "ve", silu_toks[i])
                    c.needt(vector, "ve", s_toks[i])
                    OBi = poolC[:, osl * TT:(osl + 1) * TT]
                    ins = vector.tensor_tensor(
                        OBi, al[:, i * TT:(i + 1) * TT],
                        gg[:, i * TT:(i + 1) * TT], OP.mult)
                    tok = ("ve", c.inc(ins, "ve"))
                    free_al[i] = tok
                    free_gg[i] = tok
                    c.needt(vector, "ve", xr_tok)
                    c.need(vector, "ve", "gr", c.cnt["gr"])
                    ins = vector.tensor_tensor(
                        OBi, OBi, poolA[:, xsl * TT:(xsl + 1) * TT], OP.add)
                    otok = ("ve", c.inc(ins, "ve"))
                    free_A[xsl] = otok
                    c.need(sync, "sync", "ve", otok[1])
                    ins = sync.dma_start(
                        out=outT[i * P:(i + 1) * P, t0:t0 + TT], in_=OBi)
                    c.inc(ins, gon, 16)
                    free_C[osl] = (gon, c.cnt[gon])

                for m in range(8):
                    par = cpar(t * NH + 80 + m // 2)
                    bidx = next_bank()
                    gtok = mm_group(bidx, par, (m % 2) * P, xch, xcl,
                                    extra=[y_rdy])
                    if m % 2 == 1:
                        whl_free[par] = gtok
                        load_next_half()
                    if m == 1 and t + 1 < nt:
                        xn_rdy_box[0] = emit_LN_post(t + 1)
                    if m == 7:
                        free_y_pe[0] = gtok
                    c.needt(scalar, "sc", gtok)
                    c.need(scalar, "sc", "ve", scan_toks[m][1])
                    ins = scalar.activation(
                        al[:, m * TT:(m + 1) * TT], pg[bidx][:, :], AF.Copy)
                    s_toks[m] = ("sc", c.inc(ins, "sc"))
                    bank_free[bidx] = s_toks[m]
                    if t == nt - 1:
                        emit_combine(m)

                if t < nt - 1:
                    pending_combines[0] = emit_combine

            gpsimd.wait_ge(c.h["go0"], c.cnt["go0"])
            gpsimd.wait_ge(c.h["go1"], c.cnt["go1"])
            gpsimd.wait_ge(c.h["gx"], c.cnt["gx"])
            gpsimd.wait_ge(c.h["gr"], c.cnt["gr"])
            gpsimd.wait_ge(c.h["gv"], c.cnt["gv"])
            for sg in ("gw0", "gw1", "gw2"):
                gpsimd.wait_ge(c.h[sg], c.cnt[sg])
            for s4 in ("wd0a", "wd0b", "wd1a", "wd1b", "wds0", "wds1",
                       "wds2", "wds3", "wds4"):
                sync.wait_ge(c.h[s4], c.cnt[s4])

    return nc


def _prep(L0_attn_x, L0_attn_w_in, L0_attn_w_conv, L0_attn_b_conv,
          L0_attn_w_gates, L0_attn_b_gates, L0_attn_forget_base,
          L0_attn_w_out, L0_ffn_sel_w, L0_ffn_sel_b, L0_ffn_comp_w,
          L0_ffn_comp_b):
    A = lambda v: np.ascontiguousarray(np.asarray(v, dtype=np.float32))
    x = A(L0_attn_x)
    xT = np.ascontiguousarray(x.transpose(0, 2, 1))
    w1T = A(np.asarray(L0_attn_w_in).T)
    w2T = A(np.asarray(L0_attn_w_gates).T)
    w3T = A(np.asarray(L0_attn_w_out).T)
    sel = A(L0_ffn_sel_w)
    selT = sel.T.reshape(D, 1024, 8)
    selT = selT.reshape(D, 8, 128, 8)
    selT = np.ascontiguousarray(selT.transpose(0, 1, 3, 2).reshape(D, 8192))
    cw = A(L0_ffn_comp_w)
    cmpT = cw.transpose(2, 1, 0)
    cmpT = cmpT.reshape(D, 8, 128, 8)
    cmpT = np.ascontiguousarray(cmpT.transpose(0, 1, 3, 2).reshape(D, 8192))
    vecs = np.zeros((P, VW), dtype=np.float32)
    vecs[:, C_FB:C_FB + KC] = A(L0_attn_forget_base).reshape(KC, P).T
    vecs[:, C_BCONV:C_BCONV + KC] = A(L0_attn_b_conv).reshape(KC, P).T
    wc = A(L0_attn_w_conv).reshape(D, 4).reshape(KC, P, 4)
    for k in range(KC):
        vecs[:, C_WCONV + k * 4:C_WCONV + (k + 1) * 4] = wc[k]
    bg = A(L0_attn_b_gates)
    vecs[:, C_BGF:C_BGF + KC] = bg[:D].reshape(KC, P).T
    vecs[:, C_BGI:C_BGI + KC] = bg[D:].reshape(KC, P).T
    sbr = A(L0_ffn_sel_b).reshape(1024, 8).reshape(KC, P, 8)
    cbr = A(L0_ffn_comp_b).T.reshape(KC, P, 8)
    for i in range(KC):
        for n in range(8):
            vecs[:, C_SELB + i * 8 + n] = sbr[i, :, n]
            vecs[:, C_CMPB + i * 8 + n] = cbr[i, :, n]
    for col, val in ((C_M8, -8.0), (C_M16, -16.0), (C_INVD, 1.0 / D),
                     (C_EPS, EPS), (C_MH, -0.5), (C_M1, -1.0), (C_H, 0.5)):
        vecs[:, col] = val
    return xT, w1T, w2T, w3T, selT, cmpT, vecs


def kernel(**inputs):
    xT, w1T, w2T, w3T, selT, cmpT, vecs = _prep(**inputs)
    nc = build(nt=4)
    in_maps = []
    for cid in range(8):
        in_maps.append({
            "xT": xT[cid], "w1T": w1T, "w2T": w2T, "w3T": w3T,
            "selT": selT, "cmpT": cmpT, "vecs": vecs,
        })
    res = run_bass_kernel_spmd(nc, in_maps, core_ids=list(range(8)))
    out = np.stack([np.asarray(res.results[cid]["outT"]).T
                    for cid in range(8)])
    return np.ascontiguousarray(out.astype(np.float32))
